# version 01
# baseline (speedup 1.0000x reference)
import contextlib
import numpy as np
import ml_dtypes
import concourse.bass as bass
import concourse.mybir as mybir
from concourse.bass_utils import run_bass_kernel_spmd
from concourse.aot_env import allow_ici_all_to_all

F32 = mybir.dt.float32
BF16 = mybir.dt.bfloat16
AF = mybir.ActivationFunctionType
ALU = mybir.AluOpType
AX = mybir.AxisListType
BBF16 = ml_dtypes.bfloat16

N_CORES = 8
D = 1024
T_LOC = 2048
NT = 16
NDC = 8
LN_EPS = 1e-5


class Sems:
    def __init__(self, nc, stack, names):
        self.h = {}
        self.c = {}
        for n in names:
            self.h[n] = stack.enter_context(nc.semaphore(n))
            self.c[n] = 0

    def inc(self, instr, name, v=1):
        instr.then_inc(self.h[name], v)
        self.c[name] += v
        return self.c[name]

    def wait(self, eng, name, val=None):
        v = self.c[name] if val is None else val
        if v > 0:
            eng.wait_ge(self.h[name], v)

    def step(self, eng, instr, name):
        self.inc(instr, name)
        eng.wait_ge(self.h[name], self.c[name])



def emit_ln(nc, block, S, x_res, xnT, ident, ln_stats, ln_xt, ln_scr, pt, sfx="",
            tiles=None, xnT_stride=2048):
    tiles = list(range(NT)) if tiles is None else list(tiles)
    NTL = len(tiles)
    sums = ln_stats[:, 0:16]
    sumsq = ln_stats[:, 16:32]
    mean = ln_stats[:, 32:48]
    rstd = ln_stats[:, 48:64]
    tmp = ln_stats[:, 64:80]
    lnSq, lnVar, lnSqrt, lnRstd, tr, xnTe = (
        "lnSq" + sfx, "lnVar" + sfx, "lnSqrt" + sfx, "lnRstd" + sfx,
        "tr" + sfx, "xnTe" + sfx)

    @block.vector
    def _(vector):
        for j, t in enumerate(tiles):
            i = vector.tensor_reduce(sums[:, j:j + 1], x_res[:, t * D:(t + 1) * D],
                                     AX.X, ALU.add)
        S.step(vector, i, "vch" + sfx)

    @block.scalar
    def _(scalar):
        for j, t in enumerate(tiles):
            i = scalar.activation(ln_scr[:, 0:D], x_res[:, t * D:(t + 1) * D],
                                  AF.Square, accum_out=sumsq[:, j:j + 1])
        S.inc(i, lnSq)

    @block.vector
    def _(vector):
        S.wait(vector, lnSq)
        ch = "vch" + sfx
        S.step(vector, vector.tensor_scalar(mean, sums, 1.0 / D, None, ALU.mult), ch)
        S.step(vector, vector.tensor_scalar(tmp, sumsq, 1.0 / D, None, ALU.mult), ch)
        S.step(vector, vector.tensor_mul(sumsq, mean, mean), ch)
        S.step(vector, vector.tensor_sub(tmp, tmp, sumsq), ch)
        i = vector.tensor_scalar(tmp, tmp, LN_EPS, None, ALU.add)
        S.inc(i, lnVar)

    @block.scalar
    def _(scalar):
        S.wait(scalar, lnVar)
        S.inc(scalar.sqrt(tmp, tmp), lnSqrt)

    @block.vector
    def _(vector):
        S.wait(vector, lnSqrt)
        S.inc(vector.reciprocal(rstd, tmp), lnRstd)

    base_tr = S.c[tr]
    base_xnTe = S.c[xnTe]
    for j, t in enumerate(tiles):
        @block.vector
        def _(vector, j=j, t=t):
            if j == 0:
                S.wait(vector, lnRstd)
            i = vector.tensor_scalar(ln_xt[:, (j % 2) * D:(j % 2 + 1) * D],
                                     x_res[:, t * D:(t + 1) * D],
                                     mean[:, j:j + 1], rstd[:, j:j + 1],
                                     ALU.subtract, ALU.mult)
            S.inc(i, "lnTok" + sfx)

        @block.tensor
        def _(tensor, j=j, t=t):
            S.wait(tensor, "lnTok" + sfx, S.c["lnTok" + sfx])
            if j >= 2:
                S.wait(tensor, xnTe, base_xnTe + j - 1)
            for dc in range(NDC):
                i = tensor.transpose(pt[:, (j % 2) * 1024 + dc * 128:(j % 2) * 1024 + (dc + 1) * 128],
                                     ln_xt[:, (j % 2) * D + dc * 128:(j % 2) * D + (dc + 1) * 128],
                                     ident[:])
            S.inc(i, tr)

        @block.vector
        def _(vector, j=j, t=t):
            S.wait(vector, tr)
            for dc in range(NDC):
                i = vector.tensor_copy(
                    xnT[:, dc * xnT_stride + j * 128: dc * xnT_stride + j * 128 + 128],
                    pt[:, (j % 2) * 1024 + dc * 128:(j % 2) * 1024 + (dc + 1) * 128])
            S.inc(i, xnTe)



def build_ffn0(debug=False):
    nc = bass.Bass("TRN2", target_bir_lowering=False)
    x_in = nc.dram_tensor("x", [128, NT * D], F32, kind="ExternalInput").ap()
    wfc = nc.dram_tensor("wfc", [8, 32, 128, 8, 128], F32, kind="ExternalInput").ap()
    wpr = nc.dram_tensor("wpr", [8, 2, 32, 128, 512], F32, kind="ExternalInput").ap()
    bfc = nc.dram_tensor("bfc", [128, 8 * 32], F32, kind="ExternalInput").ap()
    bpr = nc.dram_tensor("bpr", [8, 1024], F32, kind="ExternalInput").ap()
    gwT = nc.dram_tensor("gwT", [1024, 8], F32, kind="ExternalInput").ap()
    identw = nc.dram_tensor("identw", [128, 128], BF16, kind="ExternalInput").ap()
    y_out = nc.dram_tensor("y", [128, NT * D], F32, kind="ExternalOutput").ap()
    if debug:
        dbg_xnT = nc.dram_tensor("dbg_xnT", [128, 8 * 2048], BF16, kind="ExternalOutput").ap()
        dbg_gates = nc.dram_tensor("dbg_gates", [128, NT * 8], F32, kind="ExternalOutput").ap()
        dbg_glog = nc.dram_tensor("dbg_glog", [128, NT * 8], F32, kind="ExternalOutput").ap()
        dbg_HT = nc.dram_tensor("dbg_HT", [128, 32 * 512], BF16, kind="ExternalOutput").ap()

    NSB, NFC = 4, 32
    WFC_RING, WPR_RING = 4, 8

    with contextlib.ExitStack() as stack:
        sbt = lambda n, s, d: stack.enter_context(nc.sbuf_tensor(n, s, d))
        pst = lambda n, s, d: stack.enter_context(nc.psum_tensor(n, s, d))
        x_res = sbt("x_res", [128, NT * D], F32)
        xnT = sbt("xnT", [128, NDC * 2048], BF16)
        HT = sbt("HT", [128, NFC * 512], BF16)
        wfc_sb = sbt("wfc_sb", [128, WFC_RING * 8 * 128], BF16)
        wpr_sb = sbt("wpr_sb", [128, WPR_RING * 512], BF16)
        bfc_sb = sbt("bfc_sb", [128, 8 * 32], F32)
        bpr_sb = sbt("bpr_sb", [1, 8 * 1024], BF16)
        gwT_sb = sbt("gwT_sb", [128, 8 * 8], BF16)
        gates = sbt("gates", [128, NT * 8], F32)
        glog = sbt("glog", [128, NT * 8], F32)
        gscr = sbt("gscr", [128, 8], F32)
        gstat = sbt("gstat", [128, 8], F32)
        ones_row = sbt("ones_row", [1, 128], BF16)
        ident = sbt("ident", [128, 128], BF16)
        ln_stats = sbt("ln_stats", [128, 5 * 16], F32)
        ln_xt = sbt("ln_xt", [128, 2 * D], BF16)
        ln_scr = sbt("ln_scr", [128, D], F32)

        pt = pst("pt", [128, 2048], BF16)
        pm1 = [pst(f"pm1_{i}", [128, 512], F32) for i in range(2)]
        pm2 = [pst(f"pm2_{i}", [128, 512], F32) for i in range(4)]
        pg = pm2[0][:, 0:8]

        S = Sems(nc, stack, [
            "ld", "ldg", "ldv", "lnSq", "lnVar", "lnSqrt", "lnRstd", "lnTok", "tr",
            "xnTe", "gmm", "glogE", "gateF", "wfcL", "wprL", "wprUse",
            "mm1d", "geluE", "mm2d", "evacE", "st", "vch",
        ])

        with nc.Block() as block:
            @block.sync
            def _(sync):
                S.inc(sync.dma_start(out=x_res[:], in_=x_in), "ld", 16)
                S.inc(sync.dma_start(out=bfc_sb[:], in_=bfc), "ld", 16)
                S.inc(sync.dma_start(out=ident[:], in_=identw), "ld", 16)

            @block.gpsimd
            def _(gpsimd):
                S.inc(gpsimd.dma_start(out=bpr_sb[:],
                                       in_=bpr.rearrange("e d -> (e d)")[None, :]), "ldg", 16)
                S.inc(gpsimd.dma_start(
                    out=gwT_sb.ap().rearrange("p (dc e) -> p dc e", dc=8),
                    in_=gwT.rearrange("(dc p) e -> p dc e", p=128)), "ldg", 16)

            @block.vector
            def _(vector):
                vector.memset(ones_row[:], 1.0)
                S.wait(vector, "ld")
                S.wait(vector, "ldg")
                S.inc(nc.vector.engine_nop(), "ldv")

            @block.scalar
            def _(scalar):
                S.wait(scalar, "ld")
                S.wait(scalar, "ldg")

            @block.tensor
            def _(tensor):
                S.wait(tensor, "ld")
                S.wait(tensor, "ldg")

            emit_ln(nc, block, S, x_res, xnT, ident, ln_stats, ln_xt, ln_scr, pt)

            for t in range(NT):
                @block.tensor
                def _(tensor, t=t):
                    S.wait(tensor, "xnTe", t + 1)
                    if t >= 1:
                        S.wait(tensor, "glogE", t)
                    for dc in range(NDC):
                        i = tensor.matmul(pg,
                                          xnT[:, dc * 2048 + t * 128: dc * 2048 + t * 128 + 128],
                                          gwT_sb[:, dc * 8:(dc + 1) * 8],
                                          start=(dc == 0), stop=(dc == NDC - 1))
                    S.inc(i, "gmm")

                @block.vector
                def _(vector, t=t):
                    S.wait(vector, "gmm", t + 1)
                    S.inc(vector.tensor_copy(glog[:, t * 8:t * 8 + 8], pg), "glogE")

            m1 = ln_stats[:, 64:65]
            m2 = ln_stats[:, 65:66]
            nm1 = ln_stats[:, 66:67]
            d2 = ln_stats[:, 67:68]
            rdh = ln_stats[:, 68:69]

            for t in range(NT):
                @block.vector
                def _(vector, t=t):
                    S.wait(vector, "glogE", NT)
                    l = glog[:, t * 8:t * 8 + 8]
                    S.step(vector, vector.tensor_reduce(m1, l, AX.X, ALU.max), "vch")
                    S.step(vector, vector.tensor_single_scalar(gscr[:], l, m1, ALU.is_ge), "vch")
                    S.step(vector, vector.tensor_scalar(gscr[:], gscr[:], -1e30, None, ALU.mult), "vch")
                    S.step(vector, vector.tensor_add(gscr[:], gscr[:], l), "vch")
                    S.step(vector, vector.tensor_reduce(m2, gscr[:], AX.X, ALU.max), "vch")
                    S.step(vector, vector.tensor_scalar(nm1, m1, -1.0, None, ALU.mult), "vch")
                    i = vector.tensor_single_scalar(gstat[:], l, m2, ALU.is_ge)
                    S.inc(i, "gateF")

                @block.scalar
                def _(scalar, t=t):
                    S.wait(scalar, "gateF", 3 * t + 1)
                    scalar.activation(gates[:, t * 8:t * 8 + 8], glog[:, t * 8:t * 8 + 8],
                                      AF.Exp, bias=nm1)
                    i = scalar.activation(d2, m2, AF.Exp, bias=nm1)
                    S.inc(i, "gateF")

                @block.vector
                def _(vector, t=t):
                    S.wait(vector, "gateF", 3 * t + 2)
                    S.step(vector, vector.tensor_scalar(rdh, d2, 1.0, None, ALU.add), "vch")
                    S.step(vector, vector.reciprocal(rdh, rdh), "vch")
                    S.step(vector, vector.tensor_scalar(rdh, rdh, 0.5, None, ALU.mult), "vch")
                    i = vector.scalar_tensor_tensor(gates[:, t * 8:t * 8 + 8],
                                                    gstat[:], rdh,
                                                    gates[:, t * 8:t * 8 + 8],
                                                    ALU.mult, ALU.mult)
                    S.inc(i, "gateF")

            @block.gpsimd
            def _(gpsimd):
                for sbe in range(NSB * 8):
                    e = sbe % 8
                    for fc in range(NFC):
                        g = sbe * NFC + fc
                        S.wait(gpsimd, "geluE", g - (WFC_RING - 1))
                        if g > 0:
                            gpsimd.wait_ge(S.h["wfcL"], 16 * g)
                        slot = g % WFC_RING
                        S.inc(gpsimd.dma_start(
                            out=wfc_sb[:, slot * 1024:(slot + 1) * 1024],
                            in_=wfc[e, fc].rearrange("p dc c -> p (dc c)")), "wfcL", 16)
                    for dh in range(2):
                        for fc in range(NFC):
                            p = sbe * 64 + dh * NFC + fc
                            S.wait(gpsimd, "wprUse", p - (WPR_RING - 1))
                            if p > 0:
                                gpsimd.wait_ge(S.h["wprL"], 16 * p)
                            slot = p % WPR_RING
                            S.inc(gpsimd.dma_start(
                                out=wpr_sb[:, slot * 512:(slot + 1) * 512],
                                in_=wpr[e, dh, fc]), "wprL", 16)

            for sbe in range(NSB * 8):
                sb, e = divmod(sbe, 8)

                @block.tensor
                def _(tensor, sb=sb, e=e, sbe=sbe):
                    if e == 0:
                        S.wait(tensor, "xnTe", 4 * sb + 4)
                    for fc in range(NFC):
                        g = sbe * NFC + fc
                        slot = g % WFC_RING
                        tensor.wait_ge(S.h["wfcL"], 16 * (g + 1))
                        if g >= 2:
                            tensor.wait_ge(S.h["geluE"], g - 1)
                        for dc in range(NDC):
                            i = tensor.matmul(
                                pm1[g % 2][:],
                                wfc_sb[:, slot * 1024 + dc * 128: slot * 1024 + (dc + 1) * 128],
                                xnT[:, dc * 2048 + sb * 512: dc * 2048 + (sb + 1) * 512],
                                start=(dc == 0), stop=(dc == NDC - 1))
                        S.inc(i, "mm1d")

                @block.scalar
                def _(scalar, sb=sb, e=e, sbe=sbe):
                    if sbe >= 1:
                        S.wait(scalar, "mm2d", 8 * sbe)
                    for fc in range(NFC):
                        g = sbe * NFC + fc
                        scalar.wait_ge(S.h["mm1d"], g + 1)
                        i = scalar.activation(HT[:, fc * 512:(fc + 1) * 512],
                                              pm1[g % 2][:], AF.Gelu_apprx_tanh,
                                              bias=bfc_sb[:, e * 32 + fc: e * 32 + fc + 1])
                        S.inc(i, "geluE")

                @block.tensor
                def _(tensor, sb=sb, e=e, sbe=sbe):
                    tensor.wait_ge(S.h["geluE"], 32 * (sbe + 1))
                    if sbe == 0:
                        tensor.wait_ge(S.h["glogE"], NT)
                    for dh in range(2):
                        m_g = sbe * 2 + dh
                        if m_g >= 1:
                            tensor.wait_ge(S.h["evacE"], 4 * m_g)
                        for fc in range(NFC):
                            p = sbe * 64 + dh * NFC + fc
                            slot = p % WPR_RING
                            tensor.wait_ge(S.h["wprL"], 16 * (p + 1))
                            for sc in range(4):
                                i = tensor.matmul(
                                    pm2[sc][:],
                                    HT[:, fc * 512 + sc * 128: fc * 512 + (sc + 1) * 128],
                                    wpr_sb[:, slot * 512:(slot + 1) * 512],
                                    start=(fc == 0), stop=False)
                            S.inc(i, "wprUse")
                        for sc in range(4):
                            i = tensor.matmul(pm2[sc][:], ones_row[:],
                                              bpr_sb[0:1, e * 1024 + dh * 512: e * 1024 + (dh + 1) * 512],
                                              start=False, stop=True)
                            S.inc(i, "mm2d")

                @block.vector
                def _(vector, sb=sb, e=e, sbe=sbe):
                    if sbe == 0:
                        S.wait(vector, "gateF", 3 * NT)
                    for dh in range(2):
                        m_g = sbe * 2 + dh
                        for sc in range(4):
                            vector.wait_ge(S.h["mm2d"], 4 * m_g + sc + 1)
                            t = sb * 4 + sc
                            i = vector.scalar_tensor_tensor(
                                x_res[:, t * D + dh * 512: t * D + (dh + 1) * 512],
                                pm2[sc][:], gates[:, t * 8 + e: t * 8 + e + 1],
                                x_res[:, t * D + dh * 512: t * D + (dh + 1) * 512],
                                ALU.mult, ALU.add)
                            S.inc(i, "evacE")

            @block.sync
            def _(sync):
                S.wait(sync, "evacE", NSB * 8 * 8)
                S.inc(sync.dma_start(out=y_out, in_=x_res[:]), "st", 16)
                if debug:
                    S.inc(sync.dma_start(out=dbg_xnT, in_=xnT[:]), "st", 16)
                    S.inc(sync.dma_start(out=dbg_gates, in_=gates[:]), "st", 16)
                    S.inc(sync.dma_start(out=dbg_glog, in_=glog[:]), "st", 16)
                    S.inc(sync.dma_start(out=dbg_HT, in_=HT[:]), "st", 16)
                sync.wait_ge(S.h["st"], S.c["st"])

    return nc


def host_ffn0_weights(w_fc, b_fc, w_proj, b_proj, gate_w):
    wfc = np.ascontiguousarray(
        w_fc.transpose(0, 2, 1)
        .reshape(8, 8, 128, 32, 128)
        .transpose(0, 3, 2, 1, 4))
    wpr = np.ascontiguousarray(
        w_proj.transpose(0, 2, 1)
        .reshape(8, 32, 128, 2, 512)
        .transpose(0, 3, 1, 2, 4))
    bfc = np.ascontiguousarray(b_fc.reshape(8, 32, 128).transpose(2, 0, 1)
                               .reshape(128, 256))
    bpr = np.ascontiguousarray(b_proj)
    gwT = np.ascontiguousarray(gate_w.T)
    ident = np.eye(128, dtype=BBF16)
    return {"wfc": wfc, "wpr": wpr, "bfc": bfc, "bpr": bpr, "gwT": gwT,
            "identw": ident}


def run_ffn0(x_full, wdict):
    nc = build_ffn0()
    in_maps = []
    for c in range(N_CORES):
        xs = x_full[:, 512 * c:512 * (c + 1), :].reshape(T_LOC, D)
        xs = np.ascontiguousarray(xs.reshape(NT, 128, D).transpose(1, 0, 2)
                                  .reshape(128, NT * D))
        in_maps.append({"x": xs, **wdict})
    res = run_bass_kernel_spmd(nc, in_maps, core_ids=list(range(N_CORES)))
    out = np.empty((4, 4096, D), np.float32)
    for c in range(N_CORES):
        y = res.results[c]["y"].reshape(128, NT, D).transpose(1, 0, 2).reshape(T_LOC, D)
        out[:, 512 * c:512 * (c + 1), :] = y.reshape(4, 512, D)
    return out



def build_attn0():
    nc = bass.Bass("TRN2", target_bir_lowering=False)
    x_in = nc.dram_tensor("x", [128, NT * D], F32, kind="ExternalInput").ap()
    wqT = nc.dram_tensor("wqT", [1024, 1024], F32, kind="ExternalInput").ap()
    wkT = nc.dram_tensor("wkT", [1024, 1024], F32, kind="ExternalInput").ap()
    wvT = nc.dram_tensor("wvT", [1024, 1024], F32, kind="ExternalInput").ap()
    woT = nc.dram_tensor("woT", [1024, 1024], F32, kind="ExternalInput").ap()
    bo_in = nc.dram_tensor("bo", [1, 1024], F32, kind="ExternalInput").ap()
    identw = nc.dram_tensor("identw", [128, 128], BF16, kind="ExternalInput").ap()
    y_out = nc.dram_tensor("y", [128, NT * D], F32, kind="ExternalOutput").ap()

    kv_cc_in = nc.dram_tensor("kv_cc_in", [128, 32 * 130], F32)
    kv_cc_out = nc.dram_tensor("kv_cc_out", [128, 32 * 130], F32)
    AEPS = 1e-6

    with contextlib.ExitStack() as stack:
        sbt = lambda n, s, d: stack.enter_context(nc.sbuf_tensor(n, s, d))
        pst = lambda n, s, d: stack.enter_context(nc.psum_tensor(n, s, d))
        x_res = sbt("x_res", [128, NT * D], F32)
        xnT = sbt("xnT", [128, NDC * 2048], BF16)
        qf = sbt("qf", [128, NDC * 2048], BF16)
        kst = sbt("kst", [128, 1024], BF16)
        vst = sbt("vst", [128, 1024], BF16)
        wbufA = sbt("wbufA", [128, 8 * 1024], BF16)
        wbufB = sbt("wbufB", [128, 8 * 1024], BF16)
        kv_sb = sbt("kv_sb", [128, 32 * 130], F32)
        kv_bf = sbt("kv_bf", [128, 32 * 130], BF16)
        astage = sbt("astage", [128, 1024], BF16)
        bo_sb = sbt("bo_sb", [1, 1024], BF16)
        ones_col = sbt("ones_col", [128, 1], BF16)
        ones_row = sbt("ones_row", [1, 128], BF16)
        ident = sbt("ident", [128, 128], BF16)
        ln_stats = sbt("ln_stats", [128, 5 * 16], F32)
        ln_xt = sbt("ln_xt", [128, 2 * D], BF16)
        escr = sbt("escr", [128, 4 * 512], F32)
        ln_scr = escr[:, 0:D]
        dcol = sbt("dcol", [128, 4], F32)

        pt = pst("pt", [128, 2048], BF16)
        pp = [pst(f"pp_{i}", [128, 512], F32) for i in range(2)]
        pkv = pst("pkv", [128, 512], F32)
        pks = pst("pks", [128, 8], F32)
        pqk = [pst(f"pqk_{i}", [128, 130], F32) for i in range(2)]

        S = Sems(nc, stack, [
            "ld", "ldg", "wA", "wB", "lnSq", "lnVar", "lnSqrt", "lnRstd", "lnTok",
            "tr", "xnTe", "qmm", "qe1", "qe2", "qfE", "kvmm", "ke1", "ke2",
            "kvE", "kvmmB", "kvE2", "cc", "ccb", "kvbf",
            "amm", "adiv", "atr", "aTe", "omm", "oev", "st", "vch",
        ])

        with nc.Block() as block:
            @block.sync
            def _(sync):
                S.inc(sync.dma_start(out=x_res[:], in_=x_in), "ld", 16)
                S.inc(sync.dma_start(out=ident[:], in_=identw), "ld", 16)

            @block.gpsimd
            def _(gpsimd):
                for dc in range(NDC):
                    S.inc(gpsimd.dma_start(
                        out=wbufA[:, dc * 1024:(dc + 1) * 1024],
                        in_=wqT[dc * 128:(dc + 1) * 128, :]), "wA", 16)
                S.inc(gpsimd.dma_start(out=bo_sb[:], in_=bo_in), "ldg", 16)

            @block.vector
            def _(vector):
                vector.memset(ones_col[:], 1.0)
                vector.memset(ones_row[:], 1.0)
                vector.memset(kv_sb[:], 0.0)
                S.wait(vector, "ld")
                S.wait(vector, "ldg")

            @block.scalar
            def _(scalar):
                S.wait(scalar, "ld")
                S.wait(scalar, "ldg")

            @block.tensor
            def _(tensor):
                S.wait(tensor, "ld")
                S.wait(tensor, "ldg")

            emit_ln(nc, block, S, x_res, xnT, ident, ln_stats, ln_xt, ln_scr, pt)

            nq = 0
            for oc in range(NDC):
                for sb4 in range(4):
                    nq += 1

                    @block.tensor
                    def _(tensor, oc=oc, sb4=sb4, nq=nq):
                        if nq == 1:
                            S.wait(tensor, "wA")
                            S.wait(tensor, "xnTe", NT)
                        if nq > 2:
                            S.wait(tensor, "qfE", nq - 2)
                        for dc in range(NDC):
                            i = tensor.matmul(
                                pp[nq % 2][:],
                                wbufA[:, dc * 1024 + oc * 128: dc * 1024 + (oc + 1) * 128],
                                xnT[:, dc * 2048 + sb4 * 512: dc * 2048 + (sb4 + 1) * 512],
                                start=(dc == 0), stop=(dc == NDC - 1))
                        S.inc(i, "qmm")

                    @block.vector
                    def _(vector, nq=nq):
                        S.wait(vector, "qmm", nq)
                        i = vector.tensor_scalar_min(
                            escr[:, (nq % 2) * 512:(nq % 2 + 1) * 512], pp[nq % 2][:], 0.0)
                        S.inc(i, "qe1")

                    @block.scalar
                    def _(scalar, nq=nq):
                        S.wait(scalar, "qe1", nq)
                        i = scalar.activation(
                            escr[:, (2 + nq % 2) * 512:(3 + nq % 2) * 512],
                            escr[:, (nq % 2) * 512:(nq % 2 + 1) * 512], AF.Exp)
                        S.inc(i, "qe2")

                    @block.vector
                    def _(vector, oc=oc, sb4=sb4, nq=nq):
                        S.wait(vector, "qe2", nq)
                        i = vector.scalar_tensor_tensor(
                            qf[:, oc * 2048 + sb4 * 512: oc * 2048 + (sb4 + 1) * 512],
                            pp[nq % 2][:], 1.0,
                            escr[:, (2 + nq % 2) * 512:(3 + nq % 2) * 512],
                            ALU.add, ALU.max)
                        S.inc(i, "qfE")

            @block.gpsimd
            def _(gpsimd):
                gpsimd.wait_ge(S.h["qfE"], 32)
                for dc in range(NDC):
                    S.inc(gpsimd.dma_start(
                        out=wbufA[:, dc * 1024:(dc + 1) * 1024],
                        in_=wkT[dc * 128:(dc + 1) * 128, :]), "wA", 16)
                for dc in range(NDC):
                    S.inc(gpsimd.dma_start(
                        out=wbufB[:, dc * 1024:(dc + 1) * 1024],
                        in_=wvT[dc * 128:(dc + 1) * 128, :]), "wB", 16)

            nkv = 0
            nke = 0
            for t in range(NT):
                b = t // 4
                for proj in ("k", "v"):
                    wb = wbufA if proj == "k" else wbufB
                    for half in range(2):
                        nkv += 1

                        @block.tensor
                        def _(tensor, t=t, wb=wb, half=half, nkv=nkv):
                            if nkv <= 2:
                                S.wait(tensor, "wA")
                                S.wait(tensor, "wB")
                                S.wait(tensor, "qfE", 32)
                            else:
                                S.wait(tensor, "kvE", nkv - 2)
                            for dc in range(NDC):
                                i = tensor.matmul(
                                    pp[nkv % 2][:],
                                    xnT[:, dc * 2048 + t * 128: dc * 2048 + (t + 1) * 128],
                                    wb[:, dc * 1024 + half * 512: dc * 1024 + (half + 1) * 512],
                                    start=(dc == 0), stop=(dc == NDC - 1))
                            S.inc(i, "kvmm")

                        if proj == "k":
                            nke += 1

                            @block.vector
                            def _(vector, nkv=nkv):
                                S.wait(vector, "kvmm", nkv)
                                i = vector.tensor_scalar_min(
                                    escr[:, (nkv % 2) * 512:(nkv % 2 + 1) * 512],
                                    pp[nkv % 2][:], 0.0)
                                S.inc(i, "ke1")

                            @block.scalar
                            def _(scalar, nkv=nkv, nke=nke):
                                S.wait(scalar, "ke1", nke)
                                i = scalar.activation(
                                    escr[:, (2 + nkv % 2) * 512:(3 + nkv % 2) * 512],
                                    escr[:, (nkv % 2) * 512:(nkv % 2 + 1) * 512], AF.Exp)
                                S.inc(i, "ke2")

                            @block.vector
                            def _(vector, nkv=nkv, nke=nke, half=half):
                                S.wait(vector, "ke2", nke)
                                i = vector.scalar_tensor_tensor(
                                    kst[:, half * 512:(half + 1) * 512],
                                    pp[nkv % 2][:], 1.0,
                                    escr[:, (2 + nkv % 2) * 512:(3 + nkv % 2) * 512],
                                    ALU.add, ALU.max)
                                S.inc(i, "kvE")
                        else:
                            @block.vector
                            def _(vector, nkv=nkv, half=half):
                                S.wait(vector, "kvmm", nkv)
                                i = vector.tensor_copy(
                                    vst[:, half * 512:(half + 1) * 512], pp[nkv % 2][:])
                                S.inc(i, "kvE")

                @block.tensor
                def _(tensor, t=t, b=b):
                    S.wait(tensor, "kvE")
                    first = (t % 4 == 0)
                    last = (t % 4 == 3)
                    if first and t > 0:
                        S.wait(tensor, "kvE2", t // 4)
                    for h in range(16):
                        pbase = 64 * (h % 2)
                        tensor.matmul(pkv[pbase:pbase + 64, (h // 2) * 64:(h // 2) * 64 + 64],
                                      kst[:, h * 64:(h + 1) * 64],
                                      vst[:, h * 64:(h + 1) * 64],
                                      start=(first and h == 0), stop=last)
                    for h in range(16):
                        pbase = 64 * (h % 2)
                        i = tensor.matmul(pks[pbase:pbase + 64, h // 2:h // 2 + 1],
                                          kst[:, h * 64:(h + 1) * 64],
                                          ones_col[:],
                                          start=(first and h == 0), stop=last)
                    S.inc(i, "kvmmB")

                if t % 4 == 3:
                    @block.vector
                    def _(vector, b=b):
                        S.wait(vector, "kvmmB")
                        for h in range(16):
                            pbase = 64 * (h % 2)
                            blk = (b * 8 + h // 2) * 130
                            i = vector.tensor_copy(
                                kv_sb[pbase:pbase + 64, blk + 65 * (h % 2): blk + 65 * (h % 2) + 64],
                                pkv[pbase:pbase + 64, (h // 2) * 64:(h // 2) * 64 + 64])
                        for par in range(2):
                            pbase = 64 * par
                            i = vector.tensor_copy(
                                kv_sb.ap().rearrange("p (g c) -> p g c", c=130)[
                                    pbase:pbase + 64, b * 8:(b + 1) * 8,
                                    64 + 65 * par:65 + 65 * par],
                                pks[pbase:pbase + 64, 0:8].rearrange("p (g c) -> p g c", c=1))
                        S.inc(i, "kvE2")

            @block.sync
            def _(sync):
                S.wait(sync, "kvE2", 4)
                S.inc(sync.dma_start(out=kv_cc_in.ap(), in_=kv_sb[:]), "cc", 16)

            @block.gpsimd
            def _(gpsimd):
                gpsimd.wait_ge(S.h["cc"], 16)
                i = gpsimd.collective_compute(
                    "AllReduce", ALU.add,
                    replica_groups=[list(range(N_CORES))],
                    ins=[kv_cc_in.ap().opt()], outs=[kv_cc_out.ap().opt()])
                S.inc(i, "ccb")

            @block.sync
            def _(sync):
                S.wait(sync, "ccb")
                S.inc(sync.dma_start(out=kv_sb[:], in_=kv_cc_out.ap()), "cc", 16)

            @block.vector
            def _(vector):
                S.wait(vector, "cc", 32)
                S.inc(vector.tensor_copy(kv_bf[:], kv_sb[:]), "kvbf")

            na = 0
            for t in range(NT):
                b = t // 4
                for oc in range(NDC):
                    na += 1

                    @block.tensor
                    def _(tensor, t=t, b=b, oc=oc, na=na):
                        if na == 1:
                            S.wait(tensor, "kvbf")
                        if na > 2:
                            S.wait(tensor, "adiv", 2 * (na - 2))
                        blk = (b * 8 + oc) * 130
                        i = tensor.matmul(
                            pqk[na % 2][:],
                            qf[:, oc * 2048 + t * 128: oc * 2048 + (t + 1) * 128],
                            kv_bf[:, blk: blk + 130],
                            start=True, stop=True)
                        S.inc(i, "amm")

                    @block.vector
                    def _(vector, oc=oc, na=na):
                        S.wait(vector, "amm", na)
                        for par in range(2):
                            S.step(vector, vector.tensor_copy(
                                dcol[:, 2 * par:2 * par + 1],
                                pqk[na % 2][:, 64 + 65 * par:65 + 65 * par]), "vch")
                            S.step(vector, vector.tensor_scalar(
                                dcol[:, 2 * par:2 * par + 1],
                                dcol[:, 2 * par:2 * par + 1], AEPS, None, ALU.add), "vch")
                            S.step(vector, vector.reciprocal(
                                dcol[:, 2 * par + 1:2 * par + 2],
                                dcol[:, 2 * par:2 * par + 1]), "vch")
                            i = vector.tensor_scalar(
                                astage[:, (2 * oc + par) * 64:(2 * oc + par + 1) * 64],
                                pqk[na % 2][:, 65 * par:65 * par + 64],
                                dcol[:, 2 * par + 1:2 * par + 2], None, ALU.mult)
                            S.inc(i, "adiv")

                @block.tensor
                def _(tensor, t=t):
                    S.wait(tensor, "adiv")
                    if t >= 2:
                        S.wait(tensor, "aTe", t - 1)
                    for dc in range(NDC):
                        i = tensor.transpose(
                            pt[:, (t % 2) * 1024 + dc * 128:(t % 2) * 1024 + (dc + 1) * 128],
                            astage[:, dc * 128:(dc + 1) * 128], ident[:])
                    S.inc(i, "atr")

                @block.vector
                def _(vector, t=t):
                    S.wait(vector, "atr", t + 1)
                    for dc in range(NDC):
                        i = vector.tensor_copy(
                            xnT[:, dc * 2048 + t * 128: dc * 2048 + t * 128 + 128],
                            pt[:, (t % 2) * 1024 + dc * 128:(t % 2) * 1024 + (dc + 1) * 128])
                    S.inc(i, "aTe")

            @block.gpsimd
            def _(gpsimd):
                gpsimd.wait_ge(S.h["kvbf"], 1)
                for dc in range(NDC):
                    S.inc(gpsimd.dma_start(
                        out=wbufB[:, dc * 1024:(dc + 1) * 1024],
                        in_=woT[dc * 128:(dc + 1) * 128, :]), "wB", 16)

            no = 0
            for t in range(NT):
                for dh in range(2):
                    no += 1

                    @block.tensor
                    def _(tensor, t=t, dh=dh, no=no):
                        if no == 1:
                            S.wait(tensor, "wB")
                        if no > 2:
                            S.wait(tensor, "oev", no - 2)
                        S.wait(tensor, "aTe", t + 1)
                        for dc in range(NDC):
                            tensor.matmul(
                                pp[no % 2][:],
                                xnT[:, dc * 2048 + t * 128: dc * 2048 + (t + 1) * 128],
                                wbufB[:, dc * 1024 + dh * 512: dc * 1024 + (dh + 1) * 512],
                                start=(dc == 0), stop=False)
                        i = tensor.matmul(pp[no % 2][:], ones_row[:],
                                          bo_sb[0:1, dh * 512:(dh + 1) * 512],
                                          start=False, stop=True)
                        S.inc(i, "omm")

                    @block.vector
                    def _(vector, t=t, dh=dh, no=no):
                        S.wait(vector, "omm", no)
                        i = vector.tensor_add(
                            x_res[:, t * D + dh * 512: t * D + (dh + 1) * 512],
                            x_res[:, t * D + dh * 512: t * D + (dh + 1) * 512],
                            pp[no % 2][:])
                        S.inc(i, "oev")

            @block.sync
            def _(sync):
                S.wait(sync, "oev", 32)
                S.inc(sync.dma_start(out=y_out, in_=x_res[:]), "st", 16)
                sync.wait_ge(S.h["st"], 16)

    return nc


def run_attn0(x_full, wdict):
    nc = build_attn0()
    in_maps = []
    for c in range(N_CORES):
        xs = x_full[:, 512 * c:512 * (c + 1), :].reshape(T_LOC, D)
        xs = np.ascontiguousarray(xs.reshape(NT, 128, D).transpose(1, 0, 2)
                                  .reshape(128, NT * D))
        in_maps.append({"x": xs, **wdict})
    res = run_bass_kernel_spmd(nc, in_maps, core_ids=list(range(N_CORES)))
    out = np.empty((4, 4096, D), np.float32)
    for c in range(N_CORES):
        y = res.results[c]["y"].reshape(128, NT, D).transpose(1, 0, 2).reshape(T_LOC, D)
        out[:, 512 * c:512 * (c + 1), :] = y.reshape(4, 512, D)
    return out


def host_attn0_weights(wq, wk, wv, wo, bo):
    return {"wqT": np.ascontiguousarray(wq.T), "wkT": np.ascontiguousarray(wk.T),
            "wvT": np.ascontiguousarray(wv.T), "woT": np.ascontiguousarray(wo.T),
            "bo": np.ascontiguousarray(bo[None, :]),
            "identw": np.eye(128, dtype=BBF16)}



def build_ffn1():
    nc = bass.Bass("TRN2", target_bir_lowering=False)
    x_in = nc.dram_tensor("x", [128, NT * D], F32, kind="ExternalInput").ap()
    w1 = nc.dram_tensor("w1r", [64, 128, 8, 128], F32, kind="ExternalInput").ap()
    w2 = nc.dram_tensor("w2r", [2, 64, 128, 512], F32, kind="ExternalInput").ap()
    ew = nc.dram_tensor("ew", [128, 4 * 2], F32, kind="ExternalInput").ap()
    identw = nc.dram_tensor("identw", [128, 128], BF16, kind="ExternalInput").ap()
    y_out = nc.dram_tensor("y", [128, NT * D], F32, kind="ExternalOutput").ap()

    NFC = 64
    W1_RING, W2_RING = 4, 8

    with contextlib.ExitStack() as stack:
        sbt = lambda n, s, d: stack.enter_context(nc.sbuf_tensor(n, s, d))
        pst = lambda n, s, d: stack.enter_context(nc.psum_tensor(n, s, d))
        x_res = sbt("x_res", [128, NT * D], F32)
        xnT1 = sbt("xnT1", [128, NDC * 512], BF16)
        HT1 = sbt("HT1", [128, NFC * 512], BF16)
        w1_sb = sbt("w1_sb", [128, W1_RING * 8 * 128], BF16)
        w2_sb = sbt("w2_sb", [128, W2_RING * 512], BF16)
        ew_sb = sbt("ew_sb", [128, 4 * 2], F32)
        ws = sbt("ws", [128, 4], F32)
        ident = sbt("ident", [128, 128], BF16)
        ln_stats = sbt("ln_stats", [128, 5 * 16], F32)
        ln_xt = sbt("ln_xt", [128, 2 * D], BF16)
        ln_scr = sbt("ln_scr", [128, D], F32)

        pt = pst("pt", [128, 2048], BF16)
        pm1 = [pst(f"pm1_{i}", [128, 512], F32) for i in range(2)]
        pm2 = [pst(f"pm2_{i}", [128, 512], F32) for i in range(4)]

        S = Sems(nc, stack, [
            "ld", "lnSq", "lnVar", "lnSqrt", "lnRstd", "lnTok", "tr", "xnTe",
            "w1L", "w2L", "w2U", "mm1d", "geluE", "mm2d", "evacE", "wsF",
            "st", "vch",
        ])

        with nc.Block() as block:
            @block.sync
            def _(sync):
                S.inc(sync.dma_start(out=x_res[:], in_=x_in), "ld", 16)
                S.inc(sync.dma_start(out=ident[:], in_=identw), "ld", 16)
                S.inc(sync.dma_start(out=ew_sb[:], in_=ew), "ld", 16)

            @block.vector
            def _(vector):
                S.wait(vector, "ld")
                ewv = ew_sb.ap().rearrange("p (t c) -> p t c", c=2)
                wsv = ws.ap().rearrange("p (t c) -> p t c", c=1)
                S.step(vector, vector.tensor_add(
                    wsv, ewv[:, :, 0:1], ewv[:, :, 1:2]), "vch")
                i = vector.tensor_scalar(ws[:], ws[:], 0.5, None, ALU.mult)
                S.inc(i, "wsF")

            @block.scalar
            def _(scalar):
                S.wait(scalar, "ld")

            @block.tensor
            def _(tensor):
                S.wait(tensor, "ld")

            emit_ln(nc, block, S, x_res, xnT1, ident, ln_stats, ln_xt, ln_scr, pt,
                    tiles=range(4), xnT_stride=512)

            @block.gpsimd
            def _(gpsimd):
                for fc in range(NFC):
                    S.wait(gpsimd, "geluE", fc - (W1_RING - 1))
                    if fc > 0:
                        gpsimd.wait_ge(S.h["w1L"], 16 * fc)
                    slot = fc % W1_RING
                    S.inc(gpsimd.dma_start(
                        out=w1_sb[:, slot * 1024:(slot + 1) * 1024],
                        in_=w1[fc].rearrange("p dc c -> p (dc c)")), "w1L", 16)
                for dh in range(2):
                    for fc in range(NFC):
                        p = dh * NFC + fc
                        S.wait(gpsimd, "w2U", p - (W2_RING - 1))
                        if p > 0:
                            gpsimd.wait_ge(S.h["w2L"], 16 * p)
                        slot = p % W2_RING
                        S.inc(gpsimd.dma_start(
                            out=w2_sb[:, slot * 512:(slot + 1) * 512],
                            in_=w2[dh, fc]), "w2L", 16)

            for fc in range(NFC):
                @block.tensor
                def _(tensor, fc=fc):
                    if fc == 0:
                        S.wait(tensor, "xnTe", 4)
                    slot = fc % W1_RING
                    tensor.wait_ge(S.h["w1L"], 16 * (fc + 1))
                    if fc >= 2:
                        tensor.wait_ge(S.h["geluE"], fc - 1)
                    for dc in range(NDC):
                        i = tensor.matmul(
                            pm1[fc % 2][:],
                            w1_sb[:, slot * 1024 + dc * 128: slot * 1024 + (dc + 1) * 128],
                            xnT1[:, dc * 512:(dc + 1) * 512],
                            start=(dc == 0), stop=(dc == NDC - 1))
                    S.inc(i, "mm1d")

                @block.scalar
                def _(scalar, fc=fc):
                    scalar.wait_ge(S.h["mm1d"], fc + 1)
                    i = scalar.activation(HT1[:, fc * 512:(fc + 1) * 512],
                                          pm1[fc % 2][:], AF.Gelu_apprx_tanh)
                    S.inc(i, "geluE")

            for dh in range(2):
                @block.tensor
                def _(tensor, dh=dh):
                    if dh == 0:
                        tensor.wait_ge(S.h["geluE"], NFC)
                    else:
                        tensor.wait_ge(S.h["evacE"], 4)
                    for fc in range(NFC):
                        p = dh * NFC + fc
                        slot = p % W2_RING
                        tensor.wait_ge(S.h["w2L"], 16 * (p + 1))
                        for sc in range(4):
                            i = tensor.matmul(
                                pm2[sc][:],
                                HT1[:, fc * 512 + sc * 128: fc * 512 + (sc + 1) * 128],
                                w2_sb[:, slot * 512:(slot + 1) * 512],
                                start=(fc == 0), stop=(fc == NFC - 1))
                        S.inc(i, "w2U")

                @block.vector
                def _(vector, dh=dh):
                    vector.wait_ge(S.h["w2U"], NFC * (dh + 1))
                    S.wait(vector, "wsF")
                    for sc in range(4):
                        i = vector.scalar_tensor_tensor(
                            x_res[:, sc * D + dh * 512: sc * D + (dh + 1) * 512],
                            pm2[sc][:], ws[:, sc:sc + 1],
                            x_res[:, sc * D + dh * 512: sc * D + (dh + 1) * 512],
                            ALU.mult, ALU.add)
                        S.inc(i, "evacE")

            @block.sync
            def _(sync):
                S.wait(sync, "evacE", 8)
                S.inc(sync.dma_start(out=y_out, in_=x_res[:]), "st", 16)
                sync.wait_ge(S.h["st"], 16)

    return nc


def host_ffn1_weights(w1, w2):
    w1r = np.ascontiguousarray(
        w1.reshape(8, 128, 64, 128).transpose(2, 1, 0, 3))
    w2r = np.ascontiguousarray(
        w2.reshape(64, 128, 2, 512).transpose(2, 0, 1, 3))
    return {"w1r": w1r, "w2r": w2r, "identw": np.eye(128, dtype=BBF16)}


def host_ffn1_ew(expert_weights, c):
    ews = expert_weights[512 * c:512 * (c + 1), :]
    return np.ascontiguousarray(ews.reshape(4, 128, 2).transpose(1, 0, 2)
                                .reshape(128, 8))


def run_ffn1(x_full, wdict, expert_weights):
    nc = build_ffn1()
    in_maps = []
    for c in range(N_CORES):
        xs = x_full[:, 512 * c:512 * (c + 1), :].reshape(T_LOC, D)
        xs = np.ascontiguousarray(xs.reshape(NT, 128, D).transpose(1, 0, 2)
                                  .reshape(128, NT * D))
        in_maps.append({"x": xs, "ew": host_ffn1_ew(expert_weights, c), **wdict})
    res = run_bass_kernel_spmd(nc, in_maps, core_ids=list(range(N_CORES)))
    out = np.empty((4, 4096, D), np.float32)
    for c in range(N_CORES):
        y = res.results[c]["y"].reshape(128, NT, D).transpose(1, 0, 2).reshape(T_LOC, D)
        out[:, 512 * c:512 * (c + 1), :] = y.reshape(4, 512, D)
    return out



def build_attn1(debug=False):
    nc = bass.Bass("TRN2", target_bir_lowering=False)
    x_in = nc.dram_tensor("x", [128, NT * D], F32, kind="ExternalInput").ap()
    wqc = nc.dram_tensor("wqc", [1024, 512], F32, kind="ExternalInput").ap()
    wkc = nc.dram_tensor("wkc", [1024, 512], F32, kind="ExternalInput").ap()
    wvc = nc.dram_tensor("wvc", [1024, 512], F32, kind="ExternalInput").ap()
    cvq = nc.dram_tensor("cvq", [128, 4 * 4], F32, kind="ExternalInput").ap()
    cvk = nc.dram_tensor("cvk", [128, 4 * 4], F32, kind="ExternalInput").ap()
    cvv = nc.dram_tensor("cvv", [128, 4 * 4], F32, kind="ExternalInput").ap()
    wbc = nc.dram_tensor("wbc", [128, 8 * 2], F32, kind="ExternalInput").ap()
    firs = nc.dram_tensor("firs", [128, 4 * 3], F32, kind="ExternalInput").ap()
    firl = nc.dram_tensor("firl", [128, 4 * 31], F32, kind="ExternalInput").ap()
    wgl = nc.dram_tensor("wgl", [128, 8 * 16], F32, kind="ExternalInput").ap()
    bgl = nc.dram_tensor("bgl", [1, 16], F32, kind="ExternalInput").ap()
    ltemp = nc.dram_tensor("ltemp", [1, 4], F32, kind="ExternalInput").ap()
    rmsw = nc.dram_tensor("rmsw", [128, 2], F32, kind="ExternalInput").ap()
    wo_in = nc.dram_tensor("wo_n", [1024, 1024], F32, kind="ExternalInput").ap()
    masks = nc.dram_tensor("masks", [128, 3 * 32], F32, kind="ExternalInput").ap()
    identw = nc.dram_tensor("identw", [128, 128], BF16, kind="ExternalInput").ap()
    y_out = nc.dram_tensor("y", [128, NT * D], F32, kind="ExternalOutput").ap()

    if debug:
        dbg = {}
        for nmm, shp, dt in [("d_xnT", [128, 16384], BF16), ("d_qfm", [128, 8192], BF16),
                             ("d_kfm", [128, 8192], BF16), ("d_vfm", [128, 8256], BF16),
                             ("d_bcol", [128, 32], F32), ("d_ktok", [128, 8192], BF16),
                             ("d_vbtok", [128, 8192], BF16), ("d_kbfm", [128, 8192], BF16),
                             ("d_obuf", [128, 8192], BF16), ("d_glt", [128, 16], F32),
                             ("d_gsc", [128, 48], F32), ("d_ot", [128, 1024], BF16),
                             ("d_MT", [128, 32], BF16), ("d_A1T", [128, 32], BF16),
                             ("d_AT", [128, 32], BF16), ("d_M2T", [128, 32], BF16),
                             ("d_A2", [128, 32], BF16), ("d_TT", [128, 32], BF16),
                             ("d_z", [128, 256], BF16), ("d_u", [128, 256], BF16)]:
            dbg[nmm] = nc.dram_tensor(nmm, shp, dt, kind="ExternalOutput").ap()
    a1_in = nc.dram_tensor("a1_in", [8, 128, 8, 512], BF16)
    a1_out = nc.dram_tensor("a1_out", [8, 128, 8, 512], BF16)
    a2_in = nc.dram_tensor("a2_in", [8, 4, 128, 4, 512], BF16)
    a2_out = nc.dram_tensor("a2_out", [8, 4, 128, 4, 512], BF16)

    NBLK = 8
    NCH = 2
    NGRP = 32
    REPS = 1e-5
    L2EPS = 1e-6

    with contextlib.ExitStack() as stack:
        sbt = lambda n, s, d: stack.enter_context(nc.sbuf_tensor(n, s, d))
        pst = lambda n, s, d: stack.enter_context(nc.psum_tensor(n, s, d))

        xb = sbt("xb", [128, 2 * D], F32)
        xnT = sbt("xnT", [128, NDC * 2048], BF16)
        xnBb = sbt("xnBb", [128, NDC * 512], BF16)
        P_scr = sbt("P_scr", [128, 2 * 4128], BF16)
        q_fm = sbt("q_fm", [128, 2 * 4096], BF16)
        k_fm = sbt("k_fm", [128, 2 * 4096], BF16)
        v_fm = sbt("v_fm", [128, 2 * 4128], BF16)
        kb_fm = sbt("kb_fm", [128, 2 * 4096], BF16)
        k_tok = xnT[:, 0:8192]
        vb_tok = xnT[:, 8192:16384]
        o_buf = sbt("o_buf", [128, 32 * 256], BF16)
        tokst = sbt("tokst", [128, 1024], BF16)
        lsb = sbt("lsb", [128, 2 * 512], BF16)
        llb = sbt("llb", [128, 2 * 512], BF16)
        wch = sbt("wch", [128, 3 * 8 * 512], BF16)
        wb_sb = sbt("wb_sb", [128, 8 * 2], BF16)
        cv_sb = sbt("cv_sb", [128, 3 * 16], F32)
        fir_sb = sbt("fir_sb", [128, 4 * 34], F32)
        wgl_sb = sbt("wgl_sb", [128, 8 * 16], BF16)
        bgl_sb = sbt("bgl_sb", [1, 16], BF16)
        rmsw_sb = sbt("rmsw_sb", [128, 2], F32)
        itemp3 = sbt("itemp3", [128, 12], F32)
        trow = sbt("trow", [1, 16], F32)
        masks_sb = sbt("masks_sb", [128, 3 * 32], F32)
        ident = sbt("ident", [128, 128], BF16)
        ones_row = sbt("ones_row", [1, 128], BF16)
        ones_colb = sbt("ones_colb", [128, 1], BF16)
        S_f = sbt("S_f", [128, 2 * 256], F32)
        S_bf = sbt("S_bf", [128, 2 * 256], BF16)
        MT_t = sbt("MT_t", [128, 32], BF16)
        M_t = sbt("M_t", [128, 32], BF16)
        M2T_t = sbt("M2T_t", [128, 32], BF16)
        M2_t = sbt("M2_t", [128, 32], BF16)
        M4T_t = sbt("M4T_t", [128, 32], BF16)
        M4_t = sbt("M4_t", [128, 32], BF16)
        M8T_t = sbt("M8T_t", [128, 32], BF16)
        M8_t = sbt("M8_t", [128, 32], BF16)
        A1T_t = sbt("A1T_t", [128, 32], BF16)
        A2_t = sbt("A2_t", [128, 32], BF16)
        A4_t = sbt("A4_t", [128, 32], BF16)
        A8_t = sbt("A8_t", [128, 32], BF16)
        A16_t = sbt("A16_t", [128, 32], BF16)
        R_t = sbt("R_t", [128, 2 * 32], BF16)
        TT_t = sbt("TT_t", [128, 32], BF16)
        AT_t = sbt("AT_t", [128, 32], BF16)
        u_sb = sbt("u_sb", [128, 256], BF16)
        z_sb = sbt("z_sb", [128, 256], BF16)
        bcol = sbt("bcol", [128, 32], F32)
        bcol_bf = sbt("bcol_bf", [128, 32], BF16)
        l2row = sbt("l2row", [1, 512], F32)
        l2rowb = sbt("l2rowb", [1, 512], BF16)
        glt = sbt("glt", [128, 16], F32)
        gsc = sbt("gsc", [128, 48], F32)
        trowb = sbt("trowb", [1, 4], BF16)
        cst = sbt("cst_eps", [128, 2], F32)
        rmsc = sbt("rmsc", [128, 8], F32)
        ot_t = sbt("ot_t", [128, 1024], BF16)
        mx_t = sbt("mx_t", [128, 1024], BF16)
        ln_stats = sbt("ln_stats", [128, 5 * 16], F32)
        ln_xt = sbt("ln_xt", [128, 2 * D], BF16)
        wo_sb = P_scr[:, 0:8192]

        pt = pst("pt", [128, 2048], BF16)
        pp = [pst(f"pp_{i}", [128, 512], F32) for i in range(2)]
        pA = pst("pA", [128, 64], F32)
        pO = pst("pO", [128, 256], F32)
        pS = pst("pS", [128, 512], F32)
        pB = pst("pB", [128, 32], F32)
        mask_su = masks_sb[:, 0:32]
        mask_iu = masks_sb[:, 32:64]
        eye4 = masks_sb[:, 64:96]
        pZ = pp[0]
        pU = pp[1]

        S = Sems(nc, stack, [
            "xld", "wld", "wldg", "lnSq", "lnVar", "lnSqrt", "lnRstd", "lnTok", "tr",
            "xnTe", "shA", "ccA", "blkL", "pmm", "pev", "cvD", "silu",
            "l2q", "l2k", "bsig", "kbE", "scanD",
            "oev", "fir", "payT", "payD", "ccB", "cl", "clg", "clv", "cst",
            "trP", "trE", "sAm", "sAe", "sZm", "sZe", "sUm", "sUe", "sSm",
            "gmm", "gev", "mixv", "otr", "oTe", "omm", "resv", "st",
            "vch", "sch", "gch",
        ])

        def VSTEP(vector, i):
            S.step(vector, i, "vch")

        with nc.Block() as block:
            @block.sync
            def _(sync):
                S.inc(sync.dma_start(out=ident[:], in_=identw), "wld", 16)
                S.inc(sync.dma_start(out=masks_sb[:], in_=masks), "wld", 16)
                S.inc(sync.dma_start(out=cv_sb[:, 0:16], in_=cvq), "wld", 16)
                S.inc(sync.dma_start(out=cv_sb[:, 16:32], in_=cvk), "wld", 16)
                S.inc(sync.dma_start(out=cv_sb[:, 32:48], in_=cvv), "wld", 16)
                S.inc(sync.dma_start(out=fir_sb[:].rearrange("p (c t) -> p c t", t=34)[:, :, 0:3],
                                     in_=firs.rearrange("p (c t) -> p c t", t=3)), "wld", 16)
                S.inc(sync.dma_start(out=fir_sb[:].rearrange("p (c t) -> p c t", t=34)[:, :, 3:34],
                                     in_=firl.rearrange("p (c t) -> p c t", t=31)), "wld", 16)
                S.inc(sync.dma_start(out=rmsw_sb[:], in_=rmsw), "wld", 16)
                S.inc(sync.dma_start(out=trow[:, 0:4], in_=ltemp), "wld", 16)

            @block.gpsimd
            def _(gpsimd):
                for w_i, (src, off) in enumerate(((wqc, 0), (wkc, 1), (wvc, 2))):
                    for dc in range(NDC):
                        S.inc(gpsimd.dma_start(
                            out=wch[:, (w_i * 8 + dc) * 512:(w_i * 8 + dc + 1) * 512],
                            in_=src[dc * 128:(dc + 1) * 128, :]), "wldg", 16)
                S.inc(gpsimd.dma_start(out=wb_sb[:], in_=wbc), "wldg", 16)
                S.inc(gpsimd.dma_start(out=wgl_sb[:], in_=wgl), "wldg", 16)
                S.inc(gpsimd.dma_start(out=bgl_sb[:], in_=bgl), "wldg", 16)

            @block.vector
            def _(vector):
                vector.memset(ones_row[:], 1.0)
                vector.memset(ones_colb[:], 1.0)
                vector.memset(cst[:, 0:1], L2EPS)
                vector.memset(cst[:, 1:2], REPS)
                vector.memset(v_fm[:], 0.0)
                vector.memset(P_scr[:], 0.0)

            for t in range(NT):
                @block.sync
                def _(sync, t=t):
                    if t >= 2:
                        sync.wait_ge(S.h["lnSq"], t - 1)
                        sync.wait_ge(S.h["lnVar"], t - 1)
                    if t > 0:
                        sync.wait_ge(S.h["xld"], 16 * t)
                    S.inc(sync.dma_start(out=xb[:, (t % 2) * D:(t % 2 + 1) * D],
                                         in_=x_in[:, t * D:(t + 1) * D]), "xld", 16)

                @block.vector
                def _(vector, t=t):
                    vector.wait_ge(S.h["xld"], 16 * (t + 1))
                    i = vector.tensor_reduce(ln_stats[:, t:t + 1],
                                             xb[:, (t % 2) * D:(t % 2 + 1) * D],
                                             AX.X, ALU.add)
                    S.inc(i, "lnVar")

                @block.scalar
                def _(scalar, t=t):
                    scalar.wait_ge(S.h["lnVar"], t + 1)
                    i = scalar.activation(ln_xt[:, (t % 2) * D:(t % 2 + 1) * D],
                                          xb[:, (t % 2) * D:(t % 2 + 1) * D],
                                          AF.Square, accum_out=ln_stats[:, 16 + t:17 + t])
                    S.inc(i, "lnSq")

            sums = ln_stats[:, 0:16]
            sumsq = ln_stats[:, 16:32]
            mean = ln_stats[:, 32:48]
            rstd = ln_stats[:, 48:64]
            tmp = ln_stats[:, 64:80]

            @block.vector
            def _(vector):
                S.wait(vector, "lnSq", NT)
                VSTEP(vector, vector.tensor_scalar(mean, sums, 1.0 / D, None, ALU.mult))
                VSTEP(vector, vector.tensor_scalar(tmp, sumsq, 1.0 / D, None, ALU.mult))
                VSTEP(vector, vector.tensor_mul(sumsq, mean, mean))
                VSTEP(vector, vector.tensor_sub(tmp, tmp, sumsq))
                VSTEP(vector, vector.tensor_scalar(tmp, tmp, LN_EPS, None, ALU.add))
                S.inc(nc.vector.engine_nop(), "lnRstd")

            @block.scalar
            def _(scalar):
                S.wait(scalar, "lnRstd", 1)
                S.inc(scalar.sqrt(tmp, tmp), "lnSqrt")

            @block.vector
            def _(vector):
                S.wait(vector, "lnSqrt")
                VSTEP(vector, vector.reciprocal(rstd, tmp))
                S.inc(nc.vector.engine_nop(), "lnRstd")

            for t in range(NT):
                @block.sync
                def _(sync, t=t):
                    if t < 2:
                        sync.wait_ge(S.h["lnSq"], NT)
                        sync.wait_ge(S.h["lnVar"], NT)
                    else:
                        sync.wait_ge(S.h["lnTok"], t - 1)
                    sync.wait_ge(S.h["xld"], 16 * (NT + t))
                    S.inc(sync.dma_start(out=xb[:, (t % 2) * D:(t % 2 + 1) * D],
                                         in_=x_in[:, t * D:(t + 1) * D]), "xld", 16)

                @block.vector
                def _(vector, t=t):
                    if t == 0:
                        S.wait(vector, "lnRstd", 2)
                    vector.wait_ge(S.h["xld"], 16 * (NT + t + 1))
                    i = vector.tensor_scalar(ln_xt[:, (t % 2) * D:(t % 2 + 1) * D],
                                             xb[:, (t % 2) * D:(t % 2 + 1) * D],
                                             mean[:, t:t + 1], rstd[:, t:t + 1],
                                             ALU.subtract, ALU.mult)
                    S.inc(i, "lnTok")

                @block.tensor
                def _(tensor, t=t):
                    if t == 0:
                        S.wait(tensor, "wld")
                    S.wait(tensor, "lnTok", t + 1)
                    if t >= 2:
                        S.wait(tensor, "xnTe", t - 1)
                    for dc in range(NDC):
                        i = tensor.transpose(
                            pt[:, (t % 2) * 1024 + dc * 128:(t % 2) * 1024 + (dc + 1) * 128],
                            ln_xt[:, (t % 2) * D + dc * 128:(t % 2) * D + (dc + 1) * 128],
                            ident[:])
                    S.inc(i, "tr")

                @block.vector
                def _(vector, t=t):
                    S.wait(vector, "tr", t + 1)
                    for dc in range(NDC):
                        i = vector.tensor_copy(
                            xnT[:, dc * 2048 + t * 128: dc * 2048 + t * 128 + 128],
                            pt[:, (t % 2) * 1024 + dc * 128:(t % 2) * 1024 + (dc + 1) * 128])
                    S.inc(i, "xnTe")

            @block.sync
            def _(sync):
                S.wait(sync, "xnTe", NT)
                for d in range(8):
                    b = d // 2
                    S.inc(sync.dma_start(
                        out=a1_in.ap()[d],
                        in_=xnT.ap().rearrange("p (dc s) -> p dc s", s=2048)[
                            :, :, b * 512:(b + 1) * 512]), "shA", 16)

            if debug:
                @block.sync
                def _(sync):
                    S.inc(sync.dma_start(out=dbg["d_xnT"], in_=xnT[:]), "st", 16)

            @block.gpsimd
            def _(gpsimd):
                gpsimd.wait_ge(S.h["shA"], 16 * 8)
                with allow_ici_all_to_all():
                    i = gpsimd.collective_compute(
                        "AllToAll", ALU.bypass,
                        replica_groups=[list(range(N_CORES))],
                        ins=[a1_in.ap().opt()], outs=[a1_out.ap().opt()])
                S.inc(i, "ccA")

            ctr = {"gp": 0, "tp": 0, "sA": 0, "sO": 0, "sS": 0}

            def pp_group(tensor):
                ctr["gp"] += 1
                g = ctr["gp"]
                if g > 2:
                    tensor.wait_ge(S.h["pev"], g - 2)
                return pp[g % 2], g

            def pt_half(tensor):
                ctr["tp"] += 1
                n = ctr["tp"]
                if n > 2:
                    tensor.wait_ge(S.h["trE"], n - 2)
                return pt[:, (n % 2) * 1024:(n % 2 + 1) * 1024], n

            fmbuf = {"q": q_fm, "k": k_fm, "v": v_fm}
            fmoff = {"q": (0, 4096), "k": (0, 4096), "v": (32, 4128)}

            def emit_scan(ch):
                dbg_scan = debug and ch == 0
                for g in range(NGRP):
                    @block.tensor
                    def _(tensor, g=g):
                        ctr["sA"] += 1
                        if ctr["sA"] > 1:
                            tensor.wait_ge(S.h["sAe"], S.c["sAe"])
                        for j in range(4):
                            sp = (4 * g + j) * 32
                            for cc in range(2):
                                tensor.matmul(
                                    pA[32 * j:32 * (j + 1), 0:32],
                                    k_fm[:, cc * 4096 + sp: cc * 4096 + sp + 32],
                                    kb_fm[:, cc * 4096 + sp: cc * 4096 + sp + 32],
                                    start=(cc == 0), stop=(cc == 1),
                                    tile_position=(0, 32 * j))
                            for cc in range(2):
                                i = tensor.matmul(
                                    pA[32 * j:32 * (j + 1), 32:64],
                                    k_fm[:, cc * 4096 + sp: cc * 4096 + sp + 32],
                                    q_fm[:, cc * 4096 + sp: cc * 4096 + sp + 32],
                                    start=(cc == 0), stop=(cc == 1),
                                    tile_position=(0, 32 * j))
                        S.inc(i, "sAm")

                    @block.vector
                    def _(vector, g=g):
                        vector.wait_ge(S.h["sAm"], S.c["sAm"])
                        VSTEP(vector, vector.tensor_mul(MT_t[:], pA[:, 0:32], mask_su))
                        VSTEP(vector, vector.tensor_mul(AT_t[:], pA[:, 32:64], mask_iu))
                        VSTEP(vector, vector.transpose(M_t[:], MT_t[:]))
                        i = vector.tensor_add(A1T_t[:], MT_t[:], eye4[:])
                        S.inc(i, "sAe")

                    stages = [
                        ("M2T", M_t, MT_t, M2T_t, M2_t, A2_t),
                        ("M4T", M2_t, M2T_t, M4T_t, M4_t, A4_t),
                        ("M8T", M4_t, M4T_t, M8T_t, M8_t, A8_t),
                    ]
                    for sname, lhs_t, rhs_t, outT_t, out_t, outA_t in stages:
                        @block.tensor
                        def _(tensor, lhs_t=lhs_t, rhs_t=rhs_t):
                            ctr["sA"] += 1
                            tensor.wait_ge(S.h["sAe"], S.c["sAe"])
                            for j in range(4):
                                i = tensor.matmul(
                                    pA[32 * j:32 * (j + 1), 0:32],
                                    lhs_t[32 * j:32 * (j + 1), :],
                                    rhs_t[32 * j:32 * (j + 1), :],
                                    start=True, stop=True,
                                    tile_position=(32 * j, 32 * j))
                            S.inc(i, "sAm")

                        @block.vector
                        def _(vector, outT_t=outT_t, out_t=out_t, outA_t=outA_t):
                            vector.wait_ge(S.h["sAm"], S.c["sAm"])
                            VSTEP(vector, vector.tensor_copy(outT_t[:], pA[:, 0:32]))
                            VSTEP(vector, vector.transpose(out_t[:], outT_t[:]))
                            i = vector.tensor_add(outA_t[:], out_t[:], eye4[:])
                            S.inc(i, "sAe")

                    @block.tensor
                    def _(tensor):
                        ctr["sA"] += 1
                        tensor.wait_ge(S.h["sAe"], S.c["sAe"])
                        for j in range(4):
                            i = tensor.matmul(
                                pA[32 * j:32 * (j + 1), 0:32],
                                M8_t[32 * j:32 * (j + 1), :],
                                M8T_t[32 * j:32 * (j + 1), :],
                                start=True, stop=True,
                                tile_position=(32 * j, 32 * j))
                        S.inc(i, "sAm")

                    @block.vector
                    def _(vector):
                        vector.wait_ge(S.h["sAm"], S.c["sAm"])
                        VSTEP(vector, vector.tensor_add(R_t[:, 32:64], pA[:, 0:32], eye4[:]))
                        i = vector.transpose(A16_t[:], R_t[:, 32:64])
                        S.inc(i, "sAe")

                    prods = [
                        (A2_t, A1T_t, R_t, 0),
                        (A4_t, None, R_t, 32),
                        (A8_t, None, R_t, 0),
                        (A16_t, None, None, -1),
                    ]
                    rhs_off = [None, 0, 32, 0]
                    for pi2, (lhs_t, rhsA, outbuf, outoff) in enumerate(prods):
                        @block.tensor
                        def _(tensor, pi2=pi2, lhs_t=lhs_t, rhsA=rhsA):
                            ctr["sA"] += 1
                            tensor.wait_ge(S.h["sAe"], S.c["sAe"])
                            for j in range(4):
                                rhs = (rhsA[32 * j:32 * (j + 1), :] if rhsA is not None
                                       else R_t[32 * j:32 * (j + 1),
                                                rhs_off[pi2]:rhs_off[pi2] + 32])
                                i = tensor.matmul(
                                    pA[32 * j:32 * (j + 1), 0:32],
                                    lhs_t[32 * j:32 * (j + 1), :], rhs,
                                    start=True, stop=True,
                                    tile_position=(32 * j, 32 * j))
                            S.inc(i, "sAm")

                        @block.vector
                        def _(vector, pi2=pi2, outbuf=outbuf, outoff=outoff):
                            vector.wait_ge(S.h["sAm"], S.c["sAm"])
                            if outoff < 0:
                                i = vector.tensor_copy(TT_t[:], pA[:, 0:32])
                            else:
                                i = vector.tensor_copy(
                                    outbuf[:, outoff:outoff + 32], pA[:, 0:32])
                            S.inc(i, "sAe")

                    if dbg_scan and g == 0:
                        @block.sync
                        def _(sync):
                            sync.wait_ge(S.h["sAe"], S.c["sAe"])
                            for nm2, buf2 in [("d_MT", MT_t), ("d_A1T", A1T_t),
                                              ("d_AT", AT_t), ("d_M2T", M2T_t),
                                              ("d_A2", A2_t), ("d_TT", TT_t)]:
                                S.inc(sync.dma_start(out=dbg[nm2], in_=buf2[:]), "st", 16)

                    for j in range(4):
                        cidx = 4 * g + j
                        w = cidx // 4
                        sp = cidx * 32

                        @block.tensor
                        def _(tensor, g=g, j=j, cidx=cidx, w=w, sp=sp):
                            tensor.wait_ge(S.h["scanD"], S.c["scanD"])
                            if j == 0:
                                tensor.wait_ge(S.h["oev"], S.c["oev"])
                            for cc in range(2):
                                i = tensor.matmul(
                                    pZ[32 * j:32 * (j + 1), 0:256],
                                    kb_fm[:, cc * 4096 + sp: cc * 4096 + sp + 32],
                                    S_bf[:, cc * 256:(cc + 1) * 256],
                                    start=(cc == 0), stop=(cc == 1),
                                    tile_position=(0, 32 * j))
                            S.inc(i, "sZm")

                        @block.vector
                        def _(vector, j=j, cidx=cidx, w=w):
                            vector.wait_ge(S.h["sZm"], S.c["sZm"])
                            i = vector.tensor_sub(
                                z_sb[32 * j:32 * (j + 1), :],
                                vb_tok[32 * j:32 * (j + 1), w * 256:(w + 1) * 256],
                                pZ[32 * j:32 * (j + 1), 0:256])
                            S.inc(i, "sZe")

                        @block.tensor
                        def _(tensor, g=g, j=j, cidx=cidx, w=w, sp=sp):
                            tensor.wait_ge(S.h["sZe"], S.c["sZe"])
                            i = tensor.matmul(
                                pU[32 * j:32 * (j + 1), 0:256],
                                TT_t[32 * j:32 * (j + 1), :],
                                z_sb[32 * j:32 * (j + 1), :],
                                start=True, stop=True,
                                tile_position=(32 * j, 32 * j))
                            S.inc(i, "sUm")

                        @block.vector
                        def _(vector, j=j):
                            vector.wait_ge(S.h["sUm"], S.c["sUm"])
                            i = vector.tensor_copy(u_sb[32 * j:32 * (j + 1), :],
                                                   pU[32 * j:32 * (j + 1), 0:256])
                            S.inc(i, "sUe")

                        @block.tensor
                        def _(tensor, g=g, j=j, cidx=cidx, w=w, sp=sp):
                            tensor.wait_ge(S.h["sUe"], S.c["sUe"])
                            for cc in range(2):
                                tensor.matmul(
                                    pO[32 * j:32 * (j + 1), :],
                                    q_fm[:, cc * 4096 + sp: cc * 4096 + sp + 32],
                                    S_bf[:, cc * 256:(cc + 1) * 256],
                                    start=(cc == 0), stop=False,
                                    tile_position=(0, 32 * j))
                            tensor.matmul(
                                pO[32 * j:32 * (j + 1), :],
                                AT_t[32 * j:32 * (j + 1), :],
                                u_sb[32 * j:32 * (j + 1), :],
                                start=False, stop=(j == 3),
                                tile_position=(32 * j, 32 * j))
                            for cc in range(2):
                                i = tensor.matmul(
                                    pS[:, cc * 256:(cc + 1) * 256],
                                    k_tok[32 * j:32 * (j + 1),
                                          w * 256 + cc * 128: w * 256 + (cc + 1) * 128],
                                    u_sb[32 * j:32 * (j + 1), :],
                                    start=(cc == 0), stop=(cc == 1),
                                    tile_position=(32 * j, 0))
                            S.inc(i, "sSm")

                        @block.vector
                        def _(vector, j=j):
                            vector.wait_ge(S.h["sSm"], S.c["sSm"])
                            VSTEP(vector, vector.tensor_add(S_f[:], S_f[:], pS[:]))
                            i = vector.tensor_copy(S_bf[:], S_f[:])
                            S.inc(i, "scanD")

                    if dbg_scan and g == 0:
                        @block.sync
                        def _(sync):
                            sync.wait_ge(S.h["scanD"], S.c["scanD"])
                            S.inc(sync.dma_start(out=dbg["d_z"], in_=z_sb[:]), "st", 16)
                            S.inc(sync.dma_start(out=dbg["d_u"], in_=u_sb[:]), "st", 16)

                    @block.vector
                    def _(vector, g=g):
                        vector.wait_ge(S.h["sSm"], S.c["sSm"])
                        if g == 0:
                            vector.wait_ge(S.h["payD"], S.c["payD"])
                        i = vector.tensor_copy(o_buf[:, g * 256:(g + 1) * 256], pO[:])
                        S.inc(i, "oev")

            def emit_fir(ch):
                for sb8 in range(NBLK):
                    @block.vector
                    def _(vector, sb8=sb8, ch=ch):
                        if sb8 == 0:
                            vector.wait_ge(S.h["oev"], S.c["oev"])
                        for cc in range(2):
                            base = cc * 4128 + 32 + sb8 * 512
                            fw = fir_sb[:, (2 * ch + cc) * 34:(2 * ch + cc) * 34 + 34]
                            i = vector.tensor_scalar(
                                lsb[:, cc * 512:(cc + 1) * 512],
                                v_fm[:, base - 2: base - 2 + 512],
                                fw[:, 0:1], None, ALU.mult)
                            for tpp in range(1, 3):
                                i = vector.scalar_tensor_tensor(
                                    lsb[:, cc * 512:(cc + 1) * 512],
                                    v_fm[:, base - 2 + tpp: base - 2 + tpp + 512],
                                    fw[:, tpp:tpp + 1],
                                    lsb[:, cc * 512:(cc + 1) * 512], ALU.mult, ALU.add)
                            i = vector.tensor_scalar(
                                llb[:, cc * 512:(cc + 1) * 512],
                                v_fm[:, base - 30: base - 30 + 512],
                                fw[:, 3:4], None, ALU.mult)
                            for tpp in range(1, 31):
                                i = vector.scalar_tensor_tensor(
                                    llb[:, cc * 512:(cc + 1) * 512],
                                    v_fm[:, base - 30 + tpp: base - 30 + tpp + 512],
                                    fw[:, 3 + tpp:4 + tpp],
                                    llb[:, cc * 512:(cc + 1) * 512], ALU.mult, ALU.add)
                        S.inc(i, "fir")

                    for w4 in range(4):
                        @block.tensor
                        def _(tensor, sb8=sb8, w4=w4):
                            tensor.wait_ge(S.h["fir"], S.c["fir"])
                            pth, n = pt_half(tensor)
                            for cc in range(2):
                                tensor.transpose(
                                    pth[:, cc * 128:(cc + 1) * 128],
                                    lsb[:, cc * 512 + w4 * 128: cc * 512 + (w4 + 1) * 128],
                                    ident[:])
                            for cc in range(2):
                                i = tensor.transpose(
                                    pth[:, 256 + cc * 128: 256 + (cc + 1) * 128],
                                    llb[:, cc * 512 + w4 * 128: cc * 512 + (w4 + 1) * 128],
                                    ident[:])
                            S.inc(i, "trP")

                        @block.vector
                        def _(vector, sb8=sb8, w4=w4):
                            n = ctr["tp"]
                            vector.wait_ge(S.h["trP"], S.c["trP"])
                            vector.wait_ge(S.h["payD"], S.c["payD"])
                            pth = pt[:, (n % 2) * 1024:(n % 2 + 1) * 1024]
                            i = vector.tensor_copy(tokst[:, 512:1024], pth[:, 0:512])
                            S.inc(i, "trE")

                        @block.sync
                        def _(sync, sb8=sb8, w4=w4, ch=ch):
                            sync.wait_ge(S.h["trE"], S.c["trE"])
                            if S.c["payD"] > 0:
                                sync.wait_ge(S.h["payD"], S.c["payD"])
                            S.inc(sync.dma_start(
                                out=a2_in.ap()[sb8, 2][:, w4, ch * 256:(ch + 1) * 256],
                                in_=tokst[:, 512:768]), "payD", 16)
                            sync.wait_ge(S.h["payD"], S.c["payD"])
                            S.inc(sync.dma_start(
                                out=a2_in.ap()[sb8, 3][:, w4, ch * 256:(ch + 1) * 256],
                                in_=tokst[:, 768:1024]), "payD", 16)

            for ch in range(NCH):
                for pi, pn in enumerate(("q", "k", "v")):
                    for blk in range(NBLK):
                        nblk_g = (ch * 3 + pi) * NBLK + blk

                        @block.sync
                        def _(sync, blk=blk, nblk_g=nblk_g):
                            if nblk_g == 0:
                                S.wait(sync, "ccA")
                                S.wait(sync, "wldg")
                            if nblk_g >= 1:
                                sync.wait_ge(S.h["pmm"], S.c["pmm"])
                            S.inc(sync.dma_start(out=xnBb[:],
                                                 in_=a1_out.ap()[blk]), "blkL", 16)

                        for oc in range(2):
                            @block.tensor
                            def _(tensor, pi=pi, pn=pn, blk=blk, oc=oc, ch=ch):
                                tensor.wait_ge(S.h["blkL"], 16 * S.c["blkL"] // 16)
                                ppt, g = pp_group(tensor)
                                for dc in range(NDC):
                                    i = tensor.matmul(
                                        ppt[:],
                                        wch[:, (pi * 8 + dc) * 512 + ch * 256 + oc * 128:
                                            (pi * 8 + dc) * 512 + ch * 256 + (oc + 1) * 128],
                                        xnBb[:, dc * 512:(dc + 1) * 512],
                                        start=(dc == 0), stop=(dc == NDC - 1))
                                S.inc(i, "pmm")

                            @block.vector
                            def _(vector, pn=pn, blk=blk, oc=oc, g=None):
                                g = ctr["gp"]
                                vector.wait_ge(S.h["pmm"], S.c["pmm"])
                                pre, stride = fmoff[pn]
                                i = vector.tensor_copy(
                                    P_scr[:, oc * 4128 + 32 + blk * 512:
                                          oc * 4128 + 32 + (blk + 1) * 512],
                                    pp[g % 2][:])
                                S.inc(i, "pev")

                        if pi == 2:
                            @block.tensor
                            def _(tensor, blk=blk, ch=ch):
                                if blk == 0:
                                    S.wait(tensor, "bsig")
                                for sc in range(4):
                                    w = blk * 4 + sc
                                    for dc in range(NDC):
                                        i = tensor.matmul(
                                            pB[:, w:w + 1],
                                            xnBb[:, dc * 512 + sc * 128: dc * 512 + (sc + 1) * 128],
                                            wb_sb[:, dc * 2 + ch: dc * 2 + ch + 1],
                                            start=(dc == 0 and w == 0),
                                            stop=(dc == NDC - 1))
                                S.inc(i, "pmm")

                    for sb8 in range(NBLK):
                        @block.vector
                        def _(vector, pi=pi, pn=pn, sb8=sb8, ch=ch):
                            if sb8 == 0:
                                vector.wait_ge(S.h["pev"], S.c["pev"])
                            for cc in range(2):
                                base = cc * 4128 + 32 + sb8 * 512
                                tap = cv_sb[:, pi * 16 + (2 * ch + cc) * 4:
                                            pi * 16 + (2 * ch + cc) * 4 + 4]
                                i = vector.tensor_scalar(
                                    lsb[:, cc * 512:(cc + 1) * 512],
                                    P_scr[:, base - 3:base - 3 + 512],
                                    tap[:, 0:1], None, ALU.mult)
                                for tpp in range(1, 4):
                                    i = vector.scalar_tensor_tensor(
                                        lsb[:, cc * 512:(cc + 1) * 512],
                                        P_scr[:, base - 3 + tpp: base - 3 + tpp + 512],
                                        tap[:, tpp:tpp + 1],
                                        lsb[:, cc * 512:(cc + 1) * 512],
                                        ALU.mult, ALU.add)
                            S.inc(i, "cvD")

                        @block.scalar
                        def _(scalar, pn=pn, sb8=sb8):
                            scalar.wait_ge(S.h["cvD"], S.c["cvD"])
                            pre, stride = fmoff[pn]
                            for cc in range(2):
                                i = scalar.activation(
                                    fmbuf[pn][:, cc * stride + pre + sb8 * 512:
                                              cc * stride + pre + (sb8 + 1) * 512],
                                    lsb[:, cc * 512:(cc + 1) * 512], AF.Silu)
                            S.inc(i, "silu")

                        @block.vector
                        def _(vector):
                            vector.wait_ge(S.h["silu"], S.c["silu"])

                for li, pn in enumerate(("q", "k")):
                    for sb8 in range(NBLK):
                        @block.scalar
                        def _(scalar, pn=pn, sb8=sb8, li=li):
                            for cc in range(2):
                                i = scalar.activation(
                                    lsb[:, cc * 512:(cc + 1) * 512],
                                    fmbuf[pn][:, cc * 4096 + sb8 * 512: cc * 4096 + (sb8 + 1) * 512],
                                    AF.Square)
                            S.inc(i, "l2q")

                        @block.tensor
                        def _(tensor, pn=pn, sb8=sb8, li=li):
                            tensor.wait_ge(S.h["l2q"], S.c["l2q"])
                            ppt, g = pp_group(tensor)
                            for cc in range(2):
                                i = tensor.matmul(ppt[0:1, :],
                                                  ones_colb[:],
                                                  lsb[:, cc * 512:(cc + 1) * 512],
                                                  start=(cc == 0), stop=(cc == 1))
                            S.inc(i, "pmm")

                        @block.vector
                        def _(vector):
                            g = ctr["gp"]
                            vector.wait_ge(S.h["pmm"], S.c["pmm"])
                            i = vector.tensor_copy(l2row[0:1, :], pp[g % 2][0:1, :])
                            S.inc(i, "pev")

                        @block.scalar
                        def _(scalar):
                            S.wait(scalar, "pev", S.c["pev"])
                            S.step(scalar, scalar.activation(
                                l2row[0:1, :], l2row[0:1, :], AF.Ln,
                                bias=cst[0:1, 0:1]), "sch")
                            i = scalar.activation(l2rowb[0:1, :], l2row[0:1, :],
                                                  AF.Exp, scale=-0.5)
                            S.inc(i, "l2k")

                        @block.tensor
                        def _(tensor):
                            tensor.wait_ge(S.h["l2k"], S.c["l2k"])
                            ppt, g = pp_group(tensor)
                            i = tensor.matmul(ppt[:], ones_row[:], l2rowb[0:1, :],
                                              start=True, stop=True)
                            S.inc(i, "pmm")

                        @block.vector
                        def _(vector, pn=pn, sb8=sb8):
                            g = ctr["gp"]
                            vector.wait_ge(S.h["pmm"], S.c["pmm"])
                            for cc in range(2):
                                i = vector.tensor_mul(
                                    fmbuf[pn][:, cc * 4096 + sb8 * 512: cc * 4096 + (sb8 + 1) * 512],
                                    fmbuf[pn][:, cc * 4096 + sb8 * 512: cc * 4096 + (sb8 + 1) * 512],
                                    pp[g % 2][:])
                            S.inc(i, "pev")

                if debug and ch == 0:
                    @block.sync
                    def _(sync):
                        sync.wait_ge(S.h["pev"], S.c["pev"])
                        S.inc(sync.dma_start(out=dbg["d_qfm"], in_=q_fm[:]), "st", 16)
                        S.inc(sync.dma_start(out=dbg["d_kfm"], in_=k_fm[:]), "st", 16)
                        S.inc(sync.dma_start(out=dbg["d_vfm"], in_=v_fm[:]), "st", 16)

                @block.vector
                def _(vector):
                    vector.wait_ge(S.h["pmm"], S.c["pmm"])
                    VSTEP(vector, vector.tensor_copy(bcol[:], pB[:]))
                    S.inc(nc.vector.engine_nop(), "bsig")

                @block.scalar
                def _(scalar):
                    S.wait(scalar, "bsig", S.c["bsig"])
                    i = scalar.activation(bcol[:], bcol[:], AF.Exp, scale=-1.0)
                    S.inc(i, "bsig")

                @block.vector
                def _(vector):
                    S.wait(vector, "bsig", S.c["bsig"])
                    VSTEP(vector, vector.tensor_scalar(bcol[:], bcol[:], 1.0, None, ALU.add))
                    VSTEP(vector, vector.reciprocal(bcol[:], bcol[:]))
                    i = vector.tensor_copy(bcol_bf[:], bcol[:])
                    S.inc(i, "bsig")

                for w in range(32):
                    @block.tensor
                    def _(tensor, w=w, ch=ch):
                        if w == 0:
                            S.wait(tensor, "bsig", S.c["bsig"])
                        pth, n = pt_half(tensor)
                        for cc in range(2):
                            i = tensor.transpose(
                                pth[:, cc * 128:(cc + 1) * 128],
                                k_fm[:, cc * 4096 + w * 128:(cc) * 4096 + (w + 1) * 128],
                                ident[:])
                        for cc in range(2):
                            i = tensor.transpose(
                                pth[:, 256 + cc * 128: 256 + (cc + 1) * 128],
                                v_fm[:, cc * 4128 + 32 + w * 128: cc * 4128 + 32 + (w + 1) * 128],
                                ident[:])
                        S.inc(i, "trP")

                    @block.vector
                    def _(vector, w=w, ch=ch):
                        n = ctr["tp"]
                        vector.wait_ge(S.h["trP"], S.c["trP"])
                        if w > 0:
                            vector.wait_ge(S.h["payD"], S.c["payD"])
                        pth = pt[:, (n % 2) * 1024:(n % 2 + 1) * 1024]
                        for cc in range(2):
                            vector.tensor_copy(
                                k_tok[:, w * 256 + cc * 128: w * 256 + (cc + 1) * 128],
                                pth[:, cc * 128:(cc + 1) * 128])
                        for cc in range(2):
                            vector.tensor_copy(
                                tokst[:, cc * 128:(cc + 1) * 128],
                                pth[:, 256 + cc * 128: 256 + (cc + 1) * 128])
                        i = vector.tensor_scalar(
                            vb_tok[:, w * 256:(w + 1) * 256],
                            tokst[:, 0:256], bcol[:, w:w + 1], None, ALU.mult)
                        S.inc(i, "trE")

                    @block.sync
                    def _(sync, w=w, ch=ch):
                        sync.wait_ge(S.h["trE"], S.c["trE"])
                        if S.c["payD"] > 0:
                            sync.wait_ge(S.h["payD"], S.c["payD"])
                        S.inc(sync.dma_start(
                            out=a2_in.ap()[w // 4, 1][:, w % 4, ch * 256:(ch + 1) * 256],
                            in_=tokst[:, 0:256]), "payD", 16)

                for w in range(32):
                    @block.vector
                    def _(vector, w=w):
                        if w == 0:
                            vector.wait_ge(S.h["trE"], S.c["trE"])
                        i = vector.tensor_scalar(
                            tokst[:, 256:512], k_tok[:, w * 256:(w + 1) * 256],
                            bcol[:, w:w + 1], None, ALU.mult)
                        S.inc(i, "kbE")

                    @block.tensor
                    def _(tensor, w=w):
                        tensor.wait_ge(S.h["kbE"], S.c["kbE"])
                        pth, n = pt_half(tensor)
                        for cc in range(2):
                            i = tensor.transpose(
                                pth[:, cc * 128:(cc + 1) * 128],
                                tokst[:, 256 + cc * 128: 256 + (cc + 1) * 128],
                                ident[:])
                        S.inc(i, "trP")

                    @block.vector
                    def _(vector, w=w):
                        n = ctr["tp"]
                        vector.wait_ge(S.h["trP"], S.c["trP"])
                        pth = pt[:, (n % 2) * 1024:(n % 2 + 1) * 1024]
                        for cc in range(2):
                            i = vector.tensor_copy(
                                kb_fm[:, cc * 4096 + w * 128: cc * 4096 + (w + 1) * 128],
                                pth[:, cc * 128:(cc + 1) * 128])
                        S.inc(i, "trE")
                        if w == 31:
                            S.inc(nc.vector.engine_nop(), "kbE")

                if debug and ch == 0:
                    @block.sync
                    def _(sync):
                        sync.wait_ge(S.h["bsig"], S.c["bsig"])
                        S.inc(sync.dma_start(out=dbg["d_bcol"], in_=bcol[:]), "st", 16)
                        sync.wait_ge(S.h["trE"], S.c["trE"])
                        S.inc(sync.dma_start(out=dbg["d_ktok"], in_=k_tok), "st", 16)
                        S.inc(sync.dma_start(out=dbg["d_vbtok"], in_=vb_tok), "st", 16)
                        S.inc(sync.dma_start(out=dbg["d_kbfm"], in_=kb_fm[:]), "st", 16)

                @block.vector
                def _(vector):
                    vector.wait_ge(S.h["kbE"], S.c["kbE"])
                    vector.memset(S_f[:], 0.0)
                    i = vector.memset(S_bf[:], 0.0)
                    S.inc(i, "scanD")

                emit_scan(ch)

                emit_fir(ch)

                if debug and ch == 0:
                    @block.sync
                    def _(sync):
                        sync.wait_ge(S.h["oev"], S.c["oev"])
                        S.inc(sync.dma_start(out=dbg["d_obuf"], in_=o_buf[:]), "st", 16)

                @block.sync
                def _(sync, ch=ch):
                    sync.wait_ge(S.h["oev"], S.c["oev"])
                    for d in range(8):
                        sync.wait_ge(S.h["payD"], S.c["payD"])
                        S.inc(sync.dma_start(
                            out=a2_in.ap()[d, 0][:, :, ch * 256:(ch + 1) * 256],
                            in_=o_buf.ap().rearrange("p (w e) -> p w e", e=256)[
                                :, d * 4:(d + 1) * 4, :]), "payD", 16)

            @block.gpsimd
            def _(gpsimd):
                gpsimd.wait_ge(S.h["payD"], S.c["payD"])
                with allow_ici_all_to_all():
                    i = gpsimd.collective_compute(
                        "AllToAll", ALU.bypass,
                        replica_groups=[list(range(N_CORES))],
                        ins=[a2_in.ap().opt()], outs=[a2_out.ap().opt()])
                S.inc(i, "ccB")

            @block.gpsimd
            def _(gpsimd):
                gpsimd.wait_ge(S.h["silu"], S.c["silu"])
                for dc in range(NDC):
                    S.inc(gpsimd.dma_start(
                        out=wo_sb[:, dc * 1024:(dc + 1) * 1024],
                        in_=wo_in[dc * 128:(dc + 1) * 128, :]), "clg", 16)

            @block.vector
            def _(vector):
                vector.wait_ge(S.h["clg"], S.c["clg"])
                for dc in range(NDC):
                    i = vector.tensor_scalar(
                        wo_sb[:, dc * 1024:(dc + 1) * 1024],
                        wo_sb[:, dc * 1024:(dc + 1) * 1024],
                        rmsw_sb[:, dc % 2: dc % 2 + 1], None, ALU.mult)
                    S.inc(i, "clv")

            @block.scalar
            def _(scalar):
                S.wait(scalar, "wld")
                i = scalar.activation(trow[0:1, 4:8], trow[0:1, 0:4], AF.Exp)
                S.inc(i, "cst")

            @block.vector
            def _(vector):
                S.wait(vector, "cst", 1)
                VSTEP(vector, vector.reciprocal(trow[0:1, 8:12], trow[0:1, 4:8]))
                i = vector.tensor_copy(trowb[0:1, 0:4], trow[0:1, 8:12])
                S.inc(i, "cst")

            @block.tensor
            def _(tensor):
                S.wait(tensor, "cst", 2)
                S.wait(tensor, "scanD")
                S.wait(tensor, "sUe")
                ppt, g = pp_group(tensor)
                i = tensor.matmul(ppt[:, 0:4], ones_row[:], trowb[0:1, 0:4],
                                  start=True, stop=True)
                S.inc(i, "pmm")

            @block.vector
            def _(vector):
                g = ctr["gp"]
                vector.wait_ge(S.h["pmm"], S.c["pmm"])
                i = vector.tensor_copy(itemp3[:, 0:4], pp[g % 2][:, 0:4])
                S.inc(i, "pev")

            for t in range(NT):
                b, ti, par = t // 4, t % 4, t % 2
                ct_od = k_tok[:, par * 4096:par * 4096 + 1024]
                ct_v = k_tok[:, par * 4096 + 1024:par * 4096 + 2048]
                ct_ls = k_tok[:, par * 4096 + 2048:par * 4096 + 3072]
                ct_ll = k_tok[:, par * 4096 + 3072:par * 4096 + 4096]
                ct_xn = vb_tok[:, par * 1024:(par + 1) * 1024]

                @block.sync
                def _(sync, t=t, b=b, ti=ti, par=par, ct_od=ct_od, ct_v=ct_v,
                      ct_ls=ct_ls, ct_ll=ct_ll, ct_xn=ct_xn):
                    if t == 0:
                        S.wait(sync, "ccB")
                    if t >= 2:
                        sync.wait_ge(S.h["pev"], S.c["pev"])
                    for half in range(2):
                        src = 2 * b + half
                        for tix, buf in ((0, ct_od), (1, ct_v), (2, ct_ls), (3, ct_ll)):
                            S.inc(sync.dma_start(
                                out=buf[:, half * 512:(half + 1) * 512],
                                in_=a2_out.ap()[src, tix][:, ti, :]), "cl", 16)
                    S.inc(sync.dma_start(
                        out=ct_xn.rearrange("p (dc s) -> p dc s", s=128),
                        in_=a1_in.ap()[2 * b][:, :, ti * 128:(ti + 1) * 128]), "cl", 16)
                    S.inc(sync.dma_start(out=xb[:, par * D:(par + 1) * D],
                                         in_=x_in[:, t * D:(t + 1) * D]), "cl", 16)

                @block.tensor
                def _(tensor, t=t, ct_xn=ct_xn):
                    tensor.wait_ge(S.h["cl"], S.c["cl"])
                    ppt, g = pp_group(tensor)
                    for dc in range(NDC):
                        tensor.matmul(ppt[:, 0:16],
                                      ct_xn[:, dc * 128:(dc + 1) * 128],
                                      wgl_sb[:, dc * 16:(dc + 1) * 16],
                                      start=(dc == 0), stop=False)
                    i = tensor.matmul(ppt[:, 0:16], ones_row[:], bgl_sb[0:1, 0:16],
                                      start=False, stop=True)
                    S.inc(i, "gmm")

                @block.vector
                def _(vector, t=t):
                    g = ctr["gp"]
                    vector.wait_ge(S.h["gmm"], S.c["gmm"])
                    i = vector.tensor_copy(glt[:], pp[g % 2][:, 0:16])
                    S.inc(i, "pev")

                @block.scalar
                def _(scalar, t=t):
                    S.wait(scalar, "pev", S.c["pev"])
                    i = scalar.activation(gsc[:, 0:4], glt[:, 0:4], AF.Exp, scale=-1.0)
                    S.inc(i, "cst")

                @block.vector
                def _(vector, t=t):
                    S.wait(vector, "cst", S.c["cst"])
                    VSTEP(vector, vector.tensor_scalar(gsc[:, 0:4], gsc[:, 0:4],
                                                       1.0, None, ALU.add))
                    VSTEP(vector, vector.reciprocal(gsc[:, 0:4], gsc[:, 0:4]))
                    for h in range(4):
                        VSTEP(vector, vector.tensor_scalar(
                            gsc[:, 12 + 3 * h:15 + 3 * h], glt[:, 4 + 3 * h:7 + 3 * h],
                            itemp3[:, h:h + 1], None, ALU.mult))
                    VSTEP(vector, vector.tensor_reduce(
                        gsc[:, 24:28].rearrange("p (h c) -> p h c", c=1),
                        gsc[:, 12:24].rearrange("p (h c) -> p h c", c=3),
                        AX.X, ALU.max))
                    for j3 in range(3):
                        VSTEP(vector, vector.tensor_copy(
                            gsc[:, 28:40].rearrange("p (h c) -> p h c", c=3)[:, :, j3:j3 + 1],
                            gsc[:, 24:28].rearrange("p (h c) -> p h c", c=1)))
                    i = vector.tensor_sub(gsc[:, 12:24], gsc[:, 12:24], gsc[:, 28:40])
                    S.inc(i, "mixv")

                @block.scalar
                def _(scalar, t=t):
                    S.wait(scalar, "mixv", S.c["mixv"])
                    i = scalar.activation(gsc[:, 28:40], gsc[:, 12:24], AF.Exp)
                    S.inc(i, "cst")

                @block.vector
                def _(vector, t=t, ct_od=ct_od, ct_v=ct_v, ct_ls=ct_ls, ct_ll=ct_ll):
                    S.wait(vector, "cst", S.c["cst"])
                    VSTEP(vector, vector.tensor_reduce(
                        gsc[:, 24:28].rearrange("p (h c) -> p h c", c=1),
                        gsc[:, 28:40].rearrange("p (h c) -> p h c", c=3),
                        AX.X, ALU.add))
                    VSTEP(vector, vector.reciprocal(gsc[:, 24:28], gsc[:, 24:28]))
                    for j3 in range(3):
                        VSTEP(vector, vector.tensor_copy(
                            gsc[:, 12:24].rearrange("p (h c) -> p h c", c=3)[:, :, j3:j3 + 1],
                            gsc[:, 24:28].rearrange("p (h c) -> p h c", c=1)))
                    VSTEP(vector, vector.tensor_mul(gsc[:, 12:24], gsc[:, 12:24],
                                                    gsc[:, 28:40]))
                    for h in range(4):
                        p0 = gsc[:, 12 + 3 * h:13 + 3 * h]
                        p1 = gsc[:, 13 + 3 * h:14 + 3 * h]
                        p2 = gsc[:, 14 + 3 * h:15 + 3 * h]
                        sl = slice(h * 256, (h + 1) * 256)
                        vector.tensor_scalar(mx_t[:, sl], ct_v[:, sl], p0, None, ALU.mult)
                        vector.scalar_tensor_tensor(mx_t[:, sl], ct_ls[:, sl], p1,
                                                    mx_t[:, sl], ALU.mult, ALU.add)
                        vector.scalar_tensor_tensor(mx_t[:, sl], ct_ll[:, sl], p2,
                                                    mx_t[:, sl], ALU.mult, ALU.add)
                        vector.tensor_sub(ot_t[:, sl], ct_od[:, sl], mx_t[:, sl])
                        i = vector.scalar_tensor_tensor(ot_t[:, sl], ot_t[:, sl],
                                                        gsc[:, h:h + 1],
                                                        mx_t[:, sl], ALU.mult, ALU.add)
                    S.inc(i, "mixv")

                @block.scalar
                def _(scalar, t=t, par=par):
                    S.wait(scalar, "mixv", S.c["mixv"])
                    for h in range(4):
                        scalar.activation(ln_xt[:, par * D + h * 256: par * D + (h + 1) * 256],
                                          ot_t[:, h * 256:(h + 1) * 256],
                                          AF.Square, accum_out=rmsc[:, h:h + 1])
                    S.step(scalar, scalar.activation(rmsc[:, 4:8], rmsc[:, 0:4],
                                                     AF.Ln, bias=cst[:, 1:2],
                                                     scale=1.0 / 256), "sch")
                    i = scalar.activation(rmsc[:, 0:4], rmsc[:, 4:8], AF.Exp, scale=-0.5)
                    S.inc(i, "cst")

                @block.vector
                def _(vector, t=t):
                    S.wait(vector, "cst", S.c["cst"])
                    for h in range(4):
                        i = vector.tensor_scalar(ot_t[:, h * 256:(h + 1) * 256],
                                                 ot_t[:, h * 256:(h + 1) * 256],
                                                 rmsc[:, h:h + 1], None, ALU.mult)
                    S.inc(i, "otr")

                if debug and t == 0:
                    @block.sync
                    def _(sync):
                        sync.wait_ge(S.h["otr"], S.c["otr"])
                        S.inc(sync.dma_start(out=dbg["d_glt"], in_=glt[:]), "st", 16)
                        S.inc(sync.dma_start(out=dbg["d_gsc"], in_=gsc[:]), "st", 16)
                        S.inc(sync.dma_start(out=dbg["d_ot"], in_=ot_t[:]), "st", 16)

                @block.tensor
                def _(tensor, t=t):
                    tensor.wait_ge(S.h["otr"], S.c["otr"])
                    pth, n = pt_half(tensor)
                    for dc in range(NDC):
                        i = tensor.transpose(pth[:, dc * 128:(dc + 1) * 128],
                                             ot_t[:, dc * 128:(dc + 1) * 128], ident[:])
                    S.inc(i, "trP")

                @block.vector
                def _(vector, t=t):
                    n = ctr["tp"]
                    vector.wait_ge(S.h["trP"], S.c["trP"])
                    vector.wait_ge(S.h["omm"], S.c["omm"])
                    pth = pt[:, (n % 2) * 1024:(n % 2 + 1) * 1024]
                    for dc in range(NDC):
                        i = vector.tensor_copy(mx_t[:, dc * 128:(dc + 1) * 128],
                                               pth[:, dc * 128:(dc + 1) * 128])
                    S.inc(i, "trE")

                for dh in range(2):
                    @block.tensor
                    def _(tensor, t=t, dh=dh):
                        tensor.wait_ge(S.h["trE"], S.c["trE"])
                        if dh == 0 and t == 0:
                            tensor.wait_ge(S.h["clv"], S.c["clv"])
                        ppt, g = pp_group(tensor)
                        for dc in range(NDC):
                            i = tensor.matmul(
                                ppt[:],
                                mx_t[:, dc * 128:(dc + 1) * 128],
                                wo_sb[:, dc * 1024 + dh * 512: dc * 1024 + (dh + 1) * 512],
                                start=(dc == 0), stop=(dc == NDC - 1))
                        S.inc(i, "omm")

                    @block.vector
                    def _(vector, t=t, dh=dh, par=par):
                        g = ctr["gp"]
                        vector.wait_ge(S.h["omm"], S.c["omm"])
                        i = vector.tensor_add(
                            xb[:, par * D + dh * 512: par * D + (dh + 1) * 512],
                            xb[:, par * D + dh * 512: par * D + (dh + 1) * 512],
                            pp[g % 2][:])
                        S.inc(i, "pev")

                @block.sync
                def _(sync, t=t, par=par):
                    sync.wait_ge(S.h["pev"], S.c["pev"])
                    S.inc(sync.dma_start(out=y_out[:, t * D:(t + 1) * D],
                                         in_=xb[:, par * D:(par + 1) * D]), "st", 16)

            @block.sync
            def _(sync):
                sync.wait_ge(S.h["st"], S.c["st"])

    return nc


def host_attn1_weights(Wq, Wk, Wv, conv_q, conv_k, conv_v, Wb, fir_short,
                       fir_long, Wg, bg, Wl, bl, log_temp, rms_w, Wo):
    masks = np.zeros((128, 96), np.float32)
    su = -np.triu(np.ones((32, 32), np.float32), 1)
    iu = np.triu(np.ones((32, 32), np.float32), 0)
    ey = np.eye(32, dtype=np.float32)
    for j in range(4):
        masks[32 * j:32 * (j + 1), 0:32] = su
        masks[32 * j:32 * (j + 1), 32:64] = iu
        masks[32 * j:32 * (j + 1), 64:96] = ey
    wgl_np = np.concatenate([Wg, Wl], axis=1)
    return {
        "wgl": np.ascontiguousarray(wgl_np.reshape(8, 128, 16).transpose(1, 0, 2)
                                    .reshape(128, 128)),
        "bgl": np.ascontiguousarray(np.concatenate([bg, bl])[None, :]),
        "ltemp": np.ascontiguousarray(log_temp[None, :]),
        "rmsw": np.ascontiguousarray(rms_w.reshape(2, 128).T),
        "wo_n": np.ascontiguousarray(Wo),
        "masks": masks,
        "identw": np.eye(128, dtype=BBF16),
        "wbc_full": Wb, "wq_full": Wq, "wk_full": Wk, "wv_full": Wv,
        "cvq_full": conv_q, "cvk_full": conv_k, "cvv_full": conv_v,
        "firs_full": fir_short, "firl_full": fir_long,
    }


def host_attn1_core_inputs(wd, c):
    h0 = 2 * (c % 2)
    cs, ce = h0 * 256, (h0 + 2) * 256
    out = {k: v for k, v in wd.items() if not k.endswith("_full")}
    out["wqc"] = np.ascontiguousarray(wd["wq_full"][:, cs:ce])
    out["wkc"] = np.ascontiguousarray(wd["wk_full"][:, cs:ce])
    out["wvc"] = np.ascontiguousarray(wd["wv_full"][:, cs:ce])
    for name, key in (("cvq", "cvq_full"), ("cvk", "cvk_full"), ("cvv", "cvv_full")):
        sl = wd[key][cs:ce]
        out[name] = np.ascontiguousarray(sl.reshape(4, 128, 4).transpose(1, 0, 2)
                                         .reshape(128, 16))
    out["wbc"] = np.ascontiguousarray(
        wd["wbc_full"][:, h0:h0 + 2].reshape(8, 128, 2).transpose(1, 0, 2)
        .reshape(128, 16))
    fs = wd["firs_full"][h0:h0 + 2].reshape(512, 3)
    fl = wd["firl_full"][h0:h0 + 2].reshape(512, 31)
    out["firs"] = np.ascontiguousarray(fs.reshape(4, 128, 3).transpose(1, 0, 2)
                                       .reshape(128, 12))
    out["firl"] = np.ascontiguousarray(fl.reshape(4, 128, 31).transpose(1, 0, 2)
                                       .reshape(128, 124))
    return out


def run_attn1(x_full, wd):
    nc = build_attn1()
    in_maps = []
    for c in range(N_CORES):
        xs = x_full[:, 512 * c:512 * (c + 1), :].reshape(T_LOC, D)
        xs = np.ascontiguousarray(xs.reshape(NT, 128, D).transpose(1, 0, 2)
                                  .reshape(128, NT * D))
        in_maps.append({"x": xs, **host_attn1_core_inputs(wd, c)})
    res = run_bass_kernel_spmd(nc, in_maps, core_ids=list(range(N_CORES)))
    out = np.empty((4, 4096, D), np.float32)
    for c in range(N_CORES):
        y = res.results[c]["y"].reshape(128, NT, D).transpose(1, 0, 2).reshape(T_LOC, D)
        out[:, 512 * c:512 * (c + 1), :] = y.reshape(4, 512, D)
    return out


def kernel(**inputs):
    f32 = np.float32
    x = np.ascontiguousarray(np.asarray(inputs["L0_attn_x"], f32))

    wd_ffn0 = host_ffn0_weights(
        np.asarray(inputs["L0_ffn_w_fc"], f32), np.asarray(inputs["L0_ffn_b_fc"], f32),
        np.asarray(inputs["L0_ffn_w_proj"], f32), np.asarray(inputs["L0_ffn_b_proj"], f32),
        np.asarray(inputs["L0_ffn_gate_w"], f32))
    wd_attn0 = host_attn0_weights(
        np.asarray(inputs["L0_attn_wq"], f32), np.asarray(inputs["L0_attn_wk"], f32),
        np.asarray(inputs["L0_attn_wv"], f32), np.asarray(inputs["L0_attn_wo"], f32),
        np.asarray(inputs["L0_attn_bo"], f32))
    wd_ffn1 = host_ffn1_weights(
        np.asarray(inputs["L1_ffn_w1"], f32), np.asarray(inputs["L1_ffn_w2"], f32))
    ew = np.asarray(inputs["L1_ffn_expert_weights"], f32)
    wd_attn1 = host_attn1_weights(
        np.asarray(inputs["L1_attn_Wq"], f32), np.asarray(inputs["L1_attn_Wk"], f32),
        np.asarray(inputs["L1_attn_Wv"], f32), np.asarray(inputs["L1_attn_conv_q"], f32),
        np.asarray(inputs["L1_attn_conv_k"], f32), np.asarray(inputs["L1_attn_conv_v"], f32),
        np.asarray(inputs["L1_attn_Wb"], f32), np.asarray(inputs["L1_attn_fir_short"], f32),
        np.asarray(inputs["L1_attn_fir_long"], f32), np.asarray(inputs["L1_attn_Wg"], f32),
        np.asarray(inputs["L1_attn_bg"], f32), np.asarray(inputs["L1_attn_Wl"], f32),
        np.asarray(inputs["L1_attn_bl"], f32), np.asarray(inputs["L1_attn_log_temp"], f32),
        np.asarray(inputs["L1_attn_rms_w"], f32), np.asarray(inputs["L1_attn_Wo"], f32))

    xs = []
    for c in range(N_CORES):
        s = x[:, 512 * c:512 * (c + 1), :].reshape(T_LOC, D)
        xs.append(np.ascontiguousarray(
            s.reshape(NT, 128, D).transpose(1, 0, 2).reshape(128, NT * D)))

    cores = list(range(N_CORES))

    nc_ffn0 = build_ffn0()
    nc_attn0 = build_attn0()
    nc_ffn1 = build_ffn1()
    nc_attn1 = build_attn1()
    attn1_core_ins = [host_attn1_core_inputs(wd_attn1, c) for c in range(N_CORES)]
    ffn1_ews = [host_ffn1_ew(ew, c) for c in range(N_CORES)]

    def run(nc, maps):
        res = run_bass_kernel_spmd(nc, maps, core_ids=cores)
        return [np.asarray(res.results[c]["y"]) for c in range(N_CORES)]

    xs = run(nc_ffn0, [{"x": xs[c], **wd_ffn0} for c in range(N_CORES)])
    xs = run(nc_attn0, [{"x": xs[c], **wd_attn0} for c in range(N_CORES)])
    xs = run(nc_ffn0, [{"x": xs[c], **wd_ffn0} for c in range(N_CORES)])
    xs = run(nc_ffn1, [{"x": xs[c], "ew": ffn1_ews[c], **wd_ffn1} for c in range(N_CORES)])
    xs = run(nc_attn1, [{"x": xs[c], **attn1_core_ins[c]} for c in range(N_CORES)])
    xs = run(nc_ffn1, [{"x": xs[c], "ew": ffn1_ews[c], **wd_ffn1} for c in range(N_CORES)])

    out = np.empty((4, 4096, D), np.float32)
    for c in range(N_CORES):
        y = xs[c].reshape(128, NT, D).transpose(1, 0, 2).reshape(T_LOC, D)
        out[:, 512 * c:512 * (c + 1), :] = y.reshape(4, 512, D)
    return out
